# version 02
# baseline (speedup 1.0000x reference)
import numpy as np
import concourse.bass as bass
import concourse.mybir as mybir
from concourse.bass_utils import run_bass_kernel_spmd

F32 = mybir.dt.float32
BF16 = mybir.dt.bfloat16
AF = mybir.ActivationFunctionType
ALU = mybir.AluOpType
AX = mybir.AxisListType

DEBUG_TAPS = ()
STOP_STAGE = 0
WVPART = 3

T = 2048
D = 1024
KT = 8
HD = 512
H = 8
DK = 64
NCH = 4
CW = 512
NSUB = 4
SW = 128
EPS_ATT = 1e-6
EPS_LN = 1e-5

ENGS = ("sync", "tensor", "vector", "scalar", "gpsimd")


class Builder:
    def __init__(self, nc):
        self.nc = nc
        self.q = {e: [] for e in ENGS}
        self.sems = {}
        self.cnt = {}
        self.last_wait = {e: {} for e in ENGS}

    def sem(self, name):
        if name not in self.sems:
            self.sems[name] = self.nc.alloc_semaphore(name)
            self.cnt[name] = 0
        return self.sems[name]

    def emit(self, eng, fn, waits=(), inc=None, inc_n=1):
        lw = self.last_wait[eng]
        actual = []
        for w in waits:
            if w is None:
                continue
            s, v = w
            if v <= 0 or lw.get(s, 0) >= v:
                continue
            lw[s] = v
            actual.append((s, v))
        tok = None
        if inc is not None:
            self.sem(inc)
            self.cnt[inc] += inc_n
            tok = (inc, self.cnt[inc])
        sems = self.sems

        def run(e, fn=fn, actual=actual, inc=inc, inc_n=inc_n):
            for s, v in actual:
                e.wait_ge(sems[s], v)
            inst = fn(e)
            if inc is not None:
                inst.then_inc(sems[inc], inc_n)
        self.q[eng].append(run)
        return tok


def build_nc():
    nc = bass.Bass("TRN2", target_bir_lowering=False, debug=False)

    xt = nc.dram_tensor("xt", [D, T], F32, kind="ExternalInput").ap()
    wqt = nc.dram_tensor("wqt", [D, HD], F32, kind="ExternalInput").ap()
    wkt = nc.dram_tensor("wkt", [D, HD], F32, kind="ExternalInput").ap()
    wgt = nc.dram_tensor("wgt", [D, HD], F32, kind="ExternalInput").ap()
    wva = nc.dram_tensor("wva", [D, HD], F32, kind="ExternalInput").ap()
    wot = nc.dram_tensor("wot", [HD, D], F32, kind="ExternalInput").ap()
    wvt = nc.dram_tensor("wvt", [D, D], F32, kind="ExternalInput").ap()
    wrp = nc.dram_tensor("wrp", [D, 72], F32, kind="ExternalInput").ap()
    w1t = nc.dram_tensor("w1t", [D, 256], F32, kind="ExternalInput").ap()
    w2t = nc.dram_tensor("w2t", [256, D], F32, kind="ExternalInput").ap()
    b2d = nc.dram_tensor("b2d", [8, D], F32, kind="ExternalInput").ap()
    bvT = nc.dram_tensor("bvT", [128, 8], F32, kind="ExternalInput").ap()
    wgbT = nc.dram_tensor("wgbT", [128, 4], F32, kind="ExternalInput").ap()
    b1cT = nc.dram_tensor("b1cT", [128, 2], F32, kind="ExternalInput").ap()
    smallp = nc.dram_tensor("smallp", [1, 64], F32, kind="ExternalInput").ap()
    outT = nc.dram_tensor("outT", [D, T], F32, kind="ExternalOutput").ap()

    taps = {}
    tapspec = {
        "qf": [128, 2048], "kf": [128, 2048], "gt": [128, 2048],
        "vt": [128, 2080], "at0": [128, 512], "sall": [128, 260],
        "ogfm": [128, 8192], "x1": [128, 16384], "x2": [128, 16384],
        "vfb": [128, 16384], "rc": [128, 2048], "wem": [8, 2048],
        "hp": [128, 4096], "x3": [128, 16384], "qde0": [128, 256],
        "srt": [128, 64], "st1": [128, 32], "grb": [128, 32], "small": [1, 64], "srt2": [128, 64],
        "masks": [128, 1024], "lgb": [128, 8],
    }
    for t_ in DEBUG_TAPS:
        taps[t_] = nc.dram_tensor("tap_" + t_, tapspec[t_], F32 if t_ in
                                  ("sall", "x1", "x2", "rc", "x3", "masks", "lgb", "srt", "st1", "grb", "small", "srt2")
                                  else BF16, kind="ExternalOutput").ap()

    A = nc.alloc_sbuf_tensor
    X = A("X", [128, KT * T], F32)
    RING = A("RING", [128, KT * CW], BF16)
    STG = A("STG", [128, 1024], F32)
    W1REG = A("W1REG", [128, 16384], BF16)
    W2REG = A("W2REG", [128, 4096], BF16)
    W3REG = A("W3REG", [128, 8192], BF16)
    W4REG = A("W4REG", [128, 4096], BF16)
    WRPB = A("WRPB", [128, 576], BF16)
    B2B = A("B2B", [8, 1024], BF16)
    AL1 = A("AL1", [128, 12288], BF16)
    AL1F = AL1.bitcast(F32)
    GT = A("GTb", [128, 2048], BF16)
    VT = A("VTb", [128, 2080], BF16)
    OGB = A("OGB", [64, 2048], BF16)
    ATS = A("ATS", [128, 512], BF16)
    KDB = A("KDB", [128, 256], BF16)
    QDE = A("QDE", [128, 256], BF16)
    RDB = A("RDBb", [64, 512], BF16)
    SALL = A("SALL", [128, 260], F32)
    SBF = A("SBFb", [128, 520], BF16)
    E1 = A("E1", [128, 512], BF16)
    R1 = A("R1", [128, 512], BF16)
    PIECES = A("PIECES", [128, 1024], BF16)
    MASKS = A("MASKS", [128, 1024], F32)
    QD = A("QD", [128, 512], F32)
    GD2 = A("GD2", [128, 512], F32)
    IO = A("IO", [128, 128], F32)
    TRI = A("TRI", [128, 128], F32)
    IOD = A("IOD", [128, 64], F32)
    EYEF = A("EYEF", [128, 128], F32)
    EYEB = A("EYEB", [128, 128], BF16)
    LGB = A("LGB", [128, 8], F32)
    LGP = A("LGP", [128, 4], F32)
    GCP = A("GCP", [128, 4], F32)
    RDSTR = A("RDSTR", [1, 512], BF16)
    STRS = A("STRS", [1, 512], F32)
    STRB = A("STRB", [1, 1024], BF16)
    ONESC = A("ONESC", [128, 1], BF16)
    ONESR = A("ONESR", [1, 128], BF16)
    ONESRF = A("ONESRF", [1, 128], F32)
    ONEH = A("ONEH", [8, 256], BF16)
    BLK = A("BLK", [64, 8], BF16)
    BLKF = A("BLKF", [64, 8], F32)
    SMALL = A("SMALLs", [1, 64], F32)
    LGROW = A("LGROW", [1, 32], F32)
    BVS = A("BVS", [128, 8], F32)
    WGBS = A("WGBS", [128, 4], F32)
    B1CS = A("B1CS", [128, 2], F32)
    GRB = A("GRB", [128, 32], F32)
    STG2 = A("STG2", [128, 512], F32)
    CEPS = A("CEPS", [128, 1], F32)
    CEPS64 = A("CEPS64", [128, 1], F32)
    FILL = A("FILL", [128, 1], F32)
    ST1 = A("ST1", [128, 32], F32)
    SRT = A("SRT", [128, 64], F32)
    WEBF = A("WEBF", [128, 16], BF16)

    P = nc.alloc_psum_tensor
    PB = [P("pb%d" % i, [128, 512], F32) for i in range(7)]
    PTB = P("ptb", [128, 1024], BF16)

    B = Builder(nc)
    for s in ("dx", "dwa", "dwb", "dga", "dgb", "dt", "do",
              "sp", "sv", "ss", "sg"):
        B.sem(s)

    def mm(out_ap, lhsT, rhs, start, stop, waits=()):
        return B.emit("tensor",
                      lambda e, o=out_ap, l=lhsT, r=rhs, st=start, sp_=stop:
                      e.matmul(o, l, r, start=st, stop=sp_),
                      waits=waits, inc="sp")

    def tr(out_ap, in_ap, ident, waits=()):
        return B.emit("tensor",
                      lambda e, o=out_ap, i=in_ap, d=ident:
                      e.transpose(o, i, d),
                      waits=waits, inc="sp")

    def dve(fn, waits=()):
        return B.emit("vector", fn, waits=waits, inc="sv")

    def act(fn, waits=()):
        return B.emit("scalar", fn, waits=waits, inc="ss")

    def gp(fn, waits=()):
        return B.emit("gpsimd", fn, waits=waits, inc="sg")

    def dma(queue, out_ap, in_ap, sem, waits=()):
        return B.emit(queue,
                      lambda e, o=out_ap, i=in_ap: e.dma_start(out=o, in_=i),
                      waits=waits, inc=sem, inc_n=16)

    dma("gpsimd", SMALL[:, :], smallp, "dga")
    dma("gpsimd", BVS[:, :], bvT, "dga")
    dma("gpsimd", WGBS[:, :], wgbT, "dga")
    dma("gpsimd", B1CS[:, :], b1cT, "dga")
    tok_small = ("dga", B.cnt["dga"])
    for kt in range(KT):
        dma("gpsimd", X[:, kt * T:(kt + 1) * T], xt[kt * 128:(kt + 1) * 128, :], "dx")
    tok_x_all = ("dx", B.cnt["dx"])

    gp(lambda e: e.iota(IO[:, :], [[1, 128]], channel_multiplier=-1,
                        allow_small_or_imprecise_dtypes=True))
    gp(lambda e: e.iota(IOD[:, :], [[0, 64]], channel_multiplier=-1, base=127,
                        allow_small_or_imprecise_dtypes=True))
    gp(lambda e: e.tensor_single_scalar(TRI[:, :], IO[:, :], 0.0, ALU.is_ge))
    gp(lambda e: e.tensor_single_scalar(EYEF[:, :], IO[:, :], 0.0, ALU.is_equal))
    gp(lambda e: e.tensor_copy(EYEB[:, :], EYEF[:, :]))
    gp(lambda e: e.memset(ONESC[:, :], 1.0))
    gp(lambda e: e.memset(CEPS[:, :], EPS_LN))
    gp(lambda e: e.memset(CEPS64[:, :], 64.0 * EPS_LN))
    gp(lambda e: e.memset(ONESR[:, :], 1.0))
    gp(lambda e: e.memset(ONESRF[:, :], 1.0))
    gp(lambda e: e.memset(VT[:, :], 1.0))
    gp(lambda e: e.memset(SALL[:, :], 0.0))
    gp(lambda e: e.memset(SBF[:, :], 0.0))
    gp(lambda e: e.iota(BLKF[:, :], [[-8, 8]], channel_multiplier=1,
                        allow_small_or_imprecise_dtypes=True))
    gp(lambda e: e.tensor_single_scalar(BLK[:, :], BLKF[:, :], 7.0, ALU.is_le))
    gp(lambda e: e.tensor_single_scalar(BLKF[:, :], BLKF[:, :], 0.0, ALU.is_ge))
    gp(lambda e: e.tensor_mul(BLK[:, :], BLK[:, :], BLKF[:, :]))
    for ot in range(2):
        gp(lambda e, ot=ot: e.iota(ONEH[0:8, ot * 128:(ot + 1) * 128],
                                   [[1, 4], [0, 32]], channel_multiplier=-1,
                                   base=4 * ot,
                                   allow_small_or_imprecise_dtypes=True))
    gp(lambda e: e.tensor_single_scalar(ONEH[0:8, :], ONEH[0:8, :], 0.0, ALU.is_equal))
    tok_consts = ("sg", B.cnt["sg"])

    for hh in range(2):
        dma("gpsimd", STG2[0:8, 0:512], b2d[:, hh * 512:(hh + 1) * 512], "dga",
            waits=[("sg", B.cnt["sg"])] if hh else [])
        tok_b2dma = ("dga", B.cnt["dga"])
        gp(lambda e, hh=hh: e.tensor_copy(B2B[:, hh * 512:(hh + 1) * 512],
                                          STG2[0:8, 0:512]), waits=[tok_b2dma])
    tok_b2 = ("sg", B.cnt["sg"])

    act(lambda e: e.activation(LGROW[0:1, 0:8], SMALL[0:1, 32:40], AF.Sigmoid),
        waits=[tok_small])
    dve(lambda e: e.tensor_scalar_max(LGROW[0:1, 0:8], LGROW[0:1, 0:8], 1e-30),
        waits=[("ss", B.cnt["ss"])])
    act(lambda e: e.activation(LGROW[0:1, 8:16], LGROW[0:1, 0:8], AF.Ln),
        waits=[("sv", B.cnt["sv"])])
    tok_lg = ("ss", B.cnt["ss"])
    mm(PB[6][0:128, 0:8], ONESRF[0:1, :], LGROW[0:1, 8:16], True, True,
       waits=[tok_lg, tok_consts])
    act(lambda e: e.copy(LGB[:, :], PB[6][0:128, 0:8]),
        waits=[("sp", B.cnt["sp"])])
    tok_lgb = ("ss", B.cnt["ss"])
    for p in range(4):
        gp(lambda e, p=p: e.tensor_copy(LGP[0:64, p:p + 1], LGB[0:64, 2 * p:2 * p + 1]),
           waits=[tok_lgb])
        gp(lambda e, p=p: e.tensor_copy(LGP[64:128, p:p + 1],
                                        LGB[64:128, 2 * p + 1:2 * p + 2]))
    tok_lgp = ("sg", B.cnt["sg"])
    act(lambda e: e.activation(GCP[:, :], LGP[:, :], AF.Exp, scale=128.0),
        waits=[tok_lgp])
    tok_gcp = ("ss", B.cnt["ss"])
    for h in range(H):
        sl = slice(h * 128, (h + 1) * 128)
        dve(lambda e, sl=sl, h=h: e.tensor_scalar(MASKS[:, sl], IO[:, :],
                                                   LGB[:, h:h + 1], 0.0,
                                                   ALU.mult, ALU.min),
            waits=[tok_lgb, tok_consts])
        act(lambda e, sl=sl: e.activation(MASKS[:, sl], MASKS[:, sl], AF.Exp),
            waits=[("sv", B.cnt["sv"])])
        dve(lambda e, sl=sl: e.tensor_mul(MASKS[:, sl], MASKS[:, sl], TRI[:, :]),
            waits=[("ss", B.cnt["ss"])])
    tok_masks = ("sv", B.cnt["sv"])
    for p in range(4):
        sl = slice(p * 128, (p + 1) * 128)
        gp(lambda e, sl=sl: e.iota(QD[:, sl], [[1, 128]], channel_multiplier=0,
                                   base=1, allow_small_or_imprecise_dtypes=True),
           waits=[tok_lgp])
        dve(lambda e, sl=sl, p=p: e.tensor_scalar_mul(QD[:, sl], QD[:, sl],
                                                      LGP[:, p:p + 1]),
            waits=[("sg", B.cnt["sg"])])
        act(lambda e, sl=sl: e.activation(QD[:, sl], QD[:, sl], AF.Exp),
            waits=[("sv", B.cnt["sv"])])
        for half in range(2):
            h = 2 * p + half
            s2 = slice(p * 128 + half * 64, p * 128 + (half + 1) * 64)
            dve(lambda e, s2=s2, h=h: e.tensor_tensor(
                GD2[:, s2], IOD[:, :], LGB[:, h:h + 1].broadcast_to([128, 64]),
                ALU.mult), waits=[tok_lgb, tok_consts])
        act(lambda e, sl=sl: e.activation(GD2[:, sl], GD2[:, sl], AF.Exp),
            waits=[("sv", B.cnt["sv"])])
    tok_qd = ("ss", B.cnt["ss"])

    act(lambda e: e.activation(LGROW[0:1, 30:31], SMALL[0:1, 40:41], AF.Sigmoid),
        waits=[tok_small])
    tok_sa = ("ss", B.cnt["ss"])
    dve(lambda e: e.tensor_scalar_mul(LGROW[0:1, 16:24], SMALL[0:1, 16:24],
                                      LGROW[0:1, 30:31]), waits=[tok_sa])
    dve(lambda e: e.scalar_tensor_tensor(LGROW[0:1, 24:32], SMALL[0:1, 24:32],
                                         LGROW[0:1, 30:31], SMALL[0:1, 8:16],
                                         ALU.mult, ALU.add))
    tok_rows = ("sv", B.cnt["sv"])
    mm(PB[6][0:128, 32:40], ONESRF[0:1, :], SMALL[0:1, 0:8], True, False,
       waits=[tok_rows, ("ss", B.cnt["ss"])])
    mm(PB[6][0:128, 40:48], ONESRF[0:1, :], LGROW[0:1, 16:24], False, False)
    mm(PB[6][0:128, 48:56], ONESRF[0:1, :], LGROW[0:1, 24:32], False, False)
    mm(PB[6][0:128, 56:64], ONESRF[0:1, :], SMALL[0:1, 41:49], False, True)
    act(lambda e: e.copy(GRB[:, :], PB[6][0:128, 32:64]),
        waits=[("sp", B.cnt["sp"])])
    tok_grb = ("ss", B.cnt["ss"])
    sc_slot_toks = {6: [tok_grb]}

    wq_tok = {}

    def load_weight_sync(name, dram_ap, n_rows, cols, dest, dest_off, cast_eng):
        nk = n_rows // 128
        toks = []
        for k in range(nk):
            parity = load_weight_sync.par
            load_weight_sync.par ^= 1
            slot = slice(parity * 512, parity * 512 + cols)
            semn = "dwa" if parity == 0 else "dwb"
            load_weight_sync.cn[semn] += 1
            prev = load_weight_sync.prev[parity]
            dma("sync", STG[:, slot], dram_ap[k * 128:(k + 1) * 128, :], semn,
                waits=[prev] if prev else [])
            dtok = (semn, load_weight_sync.cn[semn] * 16)
            dst = dest[:, dest_off + k * cols: dest_off + (k + 1) * cols]
            if cast_eng == "scalar":
                ct = act(lambda e, d=dst, s=STG[:, slot]: e.copy(d, s), waits=[dtok])
            else:
                ct = gp(lambda e, d=dst, s=STG[:, slot]: e.tensor_copy(d, s), waits=[dtok])
            load_weight_sync.prev[parity] = ct
            toks.append(ct)
        return toks[-1]
    load_weight_sync.par = 0
    load_weight_sync.prev = {0: None, 1: None}
    load_weight_sync.cn = {"dwa": 0, "dwb": 0}

    tok_wq = load_weight_sync("wq", wqt, D, 512, W1REG, 0, "scalar")
    tok_wk = load_weight_sync("wk", wkt, D, 512, W1REG, 4096, "scalar")
    tok_wg = load_weight_sync("wg", wgt, D, 512, W1REG, 8192, "scalar")
    tok_wva = load_weight_sync("wva", wva, D, 512, W1REG, 12288, "scalar")

    def load_halves_sync(dram_ap, n_rows, rowcols, dest, dest_off):
        nk = n_rows // 128
        nh = rowcols // 512
        last = None
        for k in range(nk):
            for hh in range(nh):
                parity = load_weight_sync.par
                load_weight_sync.par ^= 1
                slot = slice(parity * 512, parity * 512 + 512)
                semn = "dwa" if parity == 0 else "dwb"
                load_weight_sync.cn[semn] += 1
                prev = load_weight_sync.prev[parity]
                dma("sync", STG[:, slot],
                    dram_ap[k * 128:(k + 1) * 128, hh * 512:(hh + 1) * 512], semn,
                    waits=[prev] if prev else [])
                dtok = (semn, load_weight_sync.cn[semn] * 16)
                dst = dest[:, dest_off + k * rowcols + hh * 512:
                           dest_off + k * rowcols + (hh + 1) * 512]
                ct = act(lambda e, d=dst, s=STG[:, slot]: e.copy(d, s), waits=[dtok])
                load_weight_sync.prev[parity] = ct
        return ct

    tok_wot = load_halves_sync(wot, HD, D, W2REG, 0)


    def load_gp(dram_ap, n_rows, rowcols, dest, dest_off, chunk):
        nk = n_rows // 128
        nh = (rowcols + chunk - 1) // chunk
        last = None
        for k in range(nk):
            for hh in range(nh):
                c0 = hh * chunk
                c1 = min(rowcols, c0 + chunk)
                parity = 0
                slot = slice(0, c1 - c0)
                semn = "dga"
                load_gp.cn[semn] += 1
                prev = load_gp.prev[parity]
                dma("gpsimd", STG2[:, slot],
                    dram_ap[k * 128:(k + 1) * 128, c0:c1], semn,
                    waits=[prev] if prev else [])
                dtok = (semn, load_gp.cn[semn] * 16)
                dst = dest[:, dest_off + k * rowcols + c0:dest_off + k * rowcols + c1]
                ct = gp(lambda e, d=dst, s=STG2[:, slot]: e.tensor_copy(d, s),
                        waits=[dtok])
                load_gp.prev[parity] = ct
                last = ct
        return last
    load_gp.par = 0
    load_gp.prev = {0: None, 1: None}
    load_gp.cn = {"dga": B.cnt["dga"] // 16, "dgb": 0}

    tok_wvt = load_halves_sync(wvt, D, D, W3REG, 0)
    tok_w1t = load_weight_sync("w1t", w1t, D, 256, W4REG, 0, "scalar")
    tok_w2t = load_halves_sync(w2t, 256, D, W4REG, 2048)
    tok_wrp = load_weight_sync("wrp", wrp, D, 72, WRPB, 0, "scalar")

    tok_stg_free = ("ss", B.cnt["ss"])

    def OGfm(sl): return AL1[:, sl]
    QFB = lambda c0, c1: AL1[:, 8192 + c0: 8192 + c1]
    KFB = lambda c0, c1: AL1[:, 10240 + c0: 10240 + c1]

    GEN_BANKS = [0, 2, 3, 6]
    tok_ring = {}
    tok_gate_applied = {}
    tok_ogb_free = {}
    prev_chunk_attn_tok = None
    prev_qf_consumed = None
    prev_gt_consumed = None
    prev_vt_consumed = None

    def bank_free(b):
        return list(sc_slot_toks.get(b, []))

    def set_bank(b, *toks):
        sc_slot_toks[b] = [t for t in toks if t is not None]
    at_par = 0
    kd_par = 0
    qde_par = 0
    rdb_par = 0
    rds_par = 0
    e1_par = 0
    prev_e1_read = {0: None, 1: None}
    prev_at_read = {0: None, 1: None}
    prev_kd_read = {0: None, 1: None}
    prev_qde_read = {0: None, 1: None}
    prev_rdb_read = {0: None, 1: None}
    prev_rds_read = {0: None, 1: None}
    prev_sco_evac = {0: None, 1: None}
    prev_sce_evac = {0: None, 1: None}
    prev_o_evac = {0: None, 1: None}
    prev_pb4_free = None
    prev_ptb_free = None
    tok_sbf = tok_consts
    tok_sall = tok_consts

    NCH_RUN = NCH if (STOP_STAGE == 0 or STOP_STAGE >= 3) else (1 if STOP_STAGE == 2 else 0)
    for c in range(NCH_RUN):
        for kt in range(KT):
            w = [tok_x_all]
            if prev_chunk_attn_tok is not None:
                w.append(prev_chunk_attn_tok)
            tok_ring[kt] = act(lambda e, kt=kt, c=c: e.copy(
                RING[:, kt * CW:(kt + 1) * CW],
                X[:, kt * T + c * CW: kt * T + (c + 1) * CW]), waits=w)

        for (proj, wtok, woff, dest) in (("q", tok_wq, 0, "QF"), ("k", tok_wk, 4096, "KF")):
            for ot in range(4):
                bank = GEN_BANKS[ot]
                pw = [wtok, tok_ring[7]] + bank_free(bank)
                for kt in range(KT):
                    lhsT = W1REG[:, woff + kt * 512 + ot * 128: woff + kt * 512 + (ot + 1) * 128]
                    rhs = RING[:, kt * CW:(kt + 1) * CW]
                    t_ = mm(PB[bank][:, 0:512], lhsT, rhs, kt == 0, kt == KT - 1,
                            waits=pw + [tok_ring[kt]] if kt == 0 else [tok_ring[kt]])
                mmtok = t_
                ep = 0
                esl = slice(0, 512)
                wE = [mmtok]
                if prev_e1_read[ep]:
                    wE.append(prev_e1_read[ep])
                act(lambda e, esl=esl, b=bank: e.activation(E1[:, esl], PB[b][:, 0:512], AF.Exp),
                    waits=wE)
                act(lambda e, esl=esl, b=bank: e.activation(R1[:, esl], PB[b][:, 0:512], AF.Relu))
                tok_evac = ("ss", B.cnt["ss"])
                dsl = slice(ot * 512, (ot + 1) * 512)
                dst = QFB(dsl.start, dsl.stop) if dest == "QF" else KFB(dsl.start, dsl.stop)
                dw_ = [tok_evac]
                if prev_qf_consumed is not None:
                    dw_.append(prev_qf_consumed)
                stok = dve(lambda e, d=dst, esl=esl: e.scalar_tensor_tensor(
                    d, E1[:, esl], 1.0, R1[:, esl], ALU.min, ALU.add), waits=dw_)
                prev_e1_read[ep] = stok
                set_bank(bank, tok_evac, stok)

        for ot in range(4):
            bank = GEN_BANKS[ot]
            pw = [tok_wg] + bank_free(bank)
            for kt in range(KT):
                lhsT = W1REG[:, 8192 + kt * 512 + ot * 128: 8192 + kt * 512 + (ot + 1) * 128]
                rhs = RING[:, kt * CW:(kt + 1) * CW]
                t_ = mm(PB[bank][:, 0:512], lhsT, rhs, kt == 0, kt == KT - 1,
                        waits=pw if kt == 0 else [])
            gw = [t_]
            if prev_gt_consumed is not None:
                gw.append(prev_gt_consumed)
            gtok = act(lambda e, ot=ot, b=bank: e.activation(
                GT[:, ot * 512:(ot + 1) * 512], PB[b][:, 0:512], AF.Sigmoid,
                bias=WGBS[:, ot:ot + 1]), waits=gw + [tok_small])
            set_bank(bank, gtok)

        for sc_ in range(NSUB):
            bank = GEN_BANKS[sc_]
            pw = [tok_wva] + bank_free(bank)
            for kt in range(KT):
                lhsT = RING[:, kt * CW + sc_ * SW: kt * CW + (sc_ + 1) * SW]
                rhs = W1REG[:, 12288 + kt * 512: 12288 + (kt + 1) * 512]
                t_ = mm(PB[bank][:, 0:512], lhsT, rhs, kt == 0, kt == KT - 1,
                        waits=pw if kt == 0 else [])
            vw = [t_]
            if prev_vt_consumed is not None:
                vw.append(prev_vt_consumed)
            out_ap = bass.AP(VT, sc_ * 520, [[VT.shape[1], 128], [130, 4], [65, 2], [1, 64]])
            in_ap = bass.AP(PB[bank], 0, [[512, 128], [128, 4], [64, 2], [1, 64]])
            vtok = dve(lambda e, o=out_ap, i=in_ap: e.tensor_copy(o, i), waits=vw)
            set_bank(bank, vtok)
        tok_vt_ready = ("sv", B.cnt["sv"])
        tok_qkg_ready = tok_vt_ready
        tok_gt_ready = ("ss", B.cnt["ss"])

        for sc_ in range(NSUB):
            s_glob = c * NSUB + sc_
            for p in range(4):
                sce_bank = 0 if (p % 2 == 0) else 6
                sco_bank = 1 if (p % 2 == 0) else 5
                qsl = slice(p * 512 + sc_ * SW, p * 512 + (sc_ + 1) * SW)
                wse = [tok_qkg_ready, tok_masks]
                if prev_sce_evac[p % 2]:
                    wse.append(prev_sce_evac[p % 2])
                mm(PB[sce_bank][:, 0:128], KFB(qsl.start, qsl.stop)[0:64, :],
                   QFB(qsl.start, qsl.stop)[0:64, :], True, True, waits=wse)
                wso = []
                if prev_sco_evac[p % 2]:
                    wso.append(prev_sco_evac[p % 2])
                tok_sc_mm = mm(PB[sco_bank][:, 0:128], KFB(qsl.start, qsl.stop)[64:128, :],
                               QFB(qsl.start, qsl.stop)[64:128, :], True, True, waits=wso)

                ap_ = at_par
                at_par ^= 1
                atsl = slice(ap_ * 256, ap_ * 256 + 256)
                wat = [tok_sc_mm]
                if prev_at_read[ap_]:
                    wat.append(prev_at_read[ap_])
                dve(lambda e, atsl=atsl, b=sce_bank, p=p: e.tensor_mul(
                    ATS[:, atsl.start:atsl.start + 128], PB[b][:, 0:128],
                    MASKS[:, (2 * p) * 128:(2 * p + 1) * 128]), waits=wat)
                tok_at_e = ("sv", B.cnt["sv"])
                dve(lambda e, atsl=atsl, b=sco_bank, p=p: e.tensor_mul(
                    ATS[:, atsl.start + 128:atsl.start + 256], PB[b][:, 0:128],
                    MASKS[:, (2 * p + 1) * 128:(2 * p + 2) * 128]))
                tok_at = ("sv", B.cnt["sv"])
                prev_sce_evac[p % 2] = tok_at_e
                prev_sco_evac[p % 2] = tok_at

                qp = qde_par
                qde_par ^= 1
                qsl2 = slice(qp * 128, (qp + 1) * 128)
                wqd = [tok_qkg_ready, tok_qd]
                if prev_qde_read[qp]:
                    wqd.append(prev_qde_read[qp])
                tok_qde = dve(lambda e, qsl=qsl, qsl2=qsl2, p=p: e.tensor_mul(
                    QDE[:, qsl2], QFB(qsl.start, qsl.stop),
                    QD[:, p * 128:(p + 1) * 128]), waits=wqd)

                ob = 2 + (p % 2)
                wo = [tok_at, tok_qde, tok_sbf, tok_vt_ready]
                if prev_o_evac[p % 2]:
                    wo.append(prev_o_evac[p % 2])
                vsl = lambda half: slice(sc_ * 520 + p * 130 + half * 65,
                                         sc_ * 520 + p * 130 + half * 65 + 65)
                mm(PB[ob][0:65, 0:128], SBF[:, p * 130: p * 130 + 65],
                   QDE[:, qsl2], True, False, waits=wo)
                mm(PB[ob][0:65, 0:128], VT[:, vsl(0)],
                   ATS[:, atsl.start:atsl.start + 128], False, False)
                mm(PB[ob][0:65, 128:256], SBF[:, p * 130 + 65: p * 130 + 130],
                   QDE[:, qsl2], False, False)
                tok_o_mm = mm(PB[ob][0:65, 128:256], VT[:, vsl(1)],
                              ATS[:, atsl.start + 128:atsl.start + 256], False, True)
                prev_at_read[ap_] = tok_o_mm
                prev_qde_read[qp] = tok_o_mm

                rp = rds_par
                rds_par ^= 1
                rsl = slice(rp * 256, (rp + 1) * 256)
                wrd = [tok_o_mm]
                if prev_rds_read[rp]:
                    wrd.append(prev_rds_read[rp])
                dve(lambda e, rsl=rsl, ob=ob: e.tensor_scalar_add(
                    STRS[0:1, rsl], PB[ob][64:65, 0:256], EPS_ATT), waits=wrd)
                def _recip(e, rsl=rsl):
                    with nc.allow_low_precision(reason="bf16 rd is a matmul operand"):
                        return e.reciprocal(RDSTR[0:1, rsl], STRS[0:1, rsl])
                tok_rd = dve(_recip)

                wbc = [tok_rd]
                if prev_pb4_free:
                    wbc.append(prev_pb4_free)
                tok_bc = mm(PB[4][0:64, 0:256], ONESR[0:1, 0:64], RDSTR[0:1, rsl],
                            True, True, waits=wbc)
                prev_rds_read[rp] = tok_bc

                rbp = rdb_par
                rdb_par ^= 1
                rbsl = slice(rbp * 256, (rbp + 1) * 256)
                wrb = [tok_bc]
                if prev_rdb_read[rbp]:
                    wrb.append(prev_rdb_read[rbp])
                tok_rdb = dve(lambda e, rbsl=rbsl: e.tensor_copy(
                    RDB[0:64, rbsl], PB[4][0:64, 0:256]), waits=wrb)
                prev_pb4_free = tok_rdb
                og_col = p * 2048 + c * CW + sc_ * SW
                dve(lambda e, ob=ob, rbsl=rbsl, og_col=og_col: e.tensor_mul(
                    AL1[0:64, og_col:og_col + 128], PB[ob][0:64, 0:128],
                    RDB[0:64, rbsl.start:rbsl.start + 128]))
                ogb_col = p * 512 + sc_ * SW
                wog = []
                if tok_ogb_free.get(p):
                    wog.append(tok_ogb_free[p])
                tok_og = dve(lambda e, ob=ob, rbsl=rbsl, ogb_col=ogb_col: e.tensor_mul(
                    OGB[0:64, ogb_col:ogb_col + 128], PB[ob][0:64, 128:256],
                    RDB[0:64, rbsl.start + 128:rbsl.start + 256]), waits=wog)
                prev_o_evac[p % 2] = tok_og
                prev_rdb_read[rbp] = tok_og

                wtr = [tok_qkg_ready]
                if prev_ptb_free:
                    wtr.append(prev_ptb_free)
                tok_tr = tr(PTB[:, 0:128], KFB(qsl.start, qsl.stop), EYEB[:, :],
                            waits=wtr)
                kp = kd_par
                kd_par ^= 1
                ksl = slice(kp * 128, (kp + 1) * 128)
                wkd = [tok_tr, tok_qd]
                if prev_kd_read[kp]:
                    wkd.append(prev_kd_read[kp])
                tok_kd = dve(lambda e, ksl=ksl, p=p: e.tensor_mul(
                    KDB[:, ksl], PTB[:, 0:128], GD2[:, p * 128:(p + 1) * 128]),
                    waits=wkd)
                prev_ptb_free = tok_kd

                wu = [tok_kd, prev_pb4_free]
                tok_u = mm(PB[4][0:128, 0:130], KDB[:, ksl],
                           VT[:, sc_ * 520 + p * 130: sc_ * 520 + (p + 1) * 130],
                           True, True, waits=wu)
                prev_kd_read[kp] = tok_u

                w_s = [tok_u, tok_gcp, tok_sall]
                dve(lambda e, p=p: e.scalar_tensor_tensor(
                    SALL[0:64, p * 65:(p + 1) * 65], SALL[0:64, p * 65:(p + 1) * 65],
                    GCP[0:64, p:p + 1], PB[4][0:64, 0:65], ALU.mult, ALU.add),
                    waits=w_s)
                dve(lambda e, p=p: e.scalar_tensor_tensor(
                    SALL[64:128, p * 65:(p + 1) * 65], SALL[64:128, p * 65:(p + 1) * 65],
                    GCP[64:128, p:p + 1], PB[4][64:128, 65:130], ALU.mult, ALU.add))
                tok_sall = ("sv", B.cnt["sv"])
                prev_pb4_free = tok_sall
                act(lambda e, p=p: e.copy(SBF[0:64, p * 130: p * 130 + 65],
                                          SALL[0:64, p * 65:(p + 1) * 65]),
                    waits=[tok_sall])
                act(lambda e, p=p: e.copy(SBF[64:128, p * 130 + 65: p * 130 + 130],
                                          SALL[64:128, p * 65:(p + 1) * 65]))
                tok_sbf = ("ss", B.cnt["ss"])

                if "at0" in taps and s_glob == 0 and p == 0:
                    dma("gpsimd", taps["at0"], ATS[:, 0:512], "do", waits=[tok_at])
                if "qde0" in taps and s_glob == 0 and p == 0:
                    dma("gpsimd", taps["qde0"], QDE[:, :], "do", waits=[tok_qde])

        for p in range(4):
            in_ap = OGB[0:64, p * 512:(p + 1) * 512]
            out_ap = AL1[64:128, p * 2048 + c * CW: p * 2048 + (c + 1) * CW]
            dma("gpsimd", out_ap, in_ap, "dt", waits=[prev_o_evac[p % 2], tok_sall])
            tok_ogb_free[p] = ("dt", B.cnt["dt"])

        tok_stitch = ("dt", B.cnt["dt"])
        for p in range(4):
            tok_gate_applied[(c, p)] = dve(
                lambda e, p=p, c=c: e.tensor_mul(
                    AL1[:, p * 2048 + c * CW: p * 2048 + (c + 1) * CW],
                    AL1[:, p * 2048 + c * CW: p * 2048 + (c + 1) * CW],
                    GT[:, p * 512:(p + 1) * 512]),
                waits=[tok_stitch, tok_gt_ready])
        prev_gt_consumed = ("sv", B.cnt["sv"])
        prev_chunk_attn_tok = ("sp", B.cnt["sp"])
        prev_qf_consumed = prev_chunk_attn_tok
        prev_vt_consumed = prev_chunk_attn_tok
        set_bank(0, prev_sce_evac[0])
        set_bank(6, prev_sce_evac[1])
        set_bank(1, prev_sco_evac[0])
        set_bank(5, prev_sco_evac[1])
        set_bank(2, prev_o_evac[0])
        set_bank(3, prev_o_evac[1])
        set_bank(4, tok_sall)

        if "qf" in taps and c == 0:
            dma("gpsimd", taps["qf"], AL1[:, 8192:10240], "do", waits=[tok_qkg_ready])
        if "kf" in taps and c == 0:
            dma("gpsimd", taps["kf"], AL1[:, 10240:12288], "do", waits=[tok_qkg_ready])
        if "gt" in taps and c == 0:
            dma("gpsimd", taps["gt"], GT[:, :], "do", waits=[tok_gt_ready])
        if "vt" in taps and c == 0:
            dma("gpsimd", taps["vt"], VT[:, :], "do", waits=[tok_vt_ready])
        if "sall" in taps and c == 0:
            dma("gpsimd", taps["sall"], SALL[:, :], "do", waits=[tok_sall])

    tok_og_done = ("sv", B.cnt["sv"])
    if "grb" in taps:
        dma("gpsimd", taps["grb"], GRB[:, :], "do", waits=[tok_grb])
        dma("gpsimd", taps["small"], SMALL[:, :], "do", waits=[tok_grb])
    if "masks" in taps:
        dma("gpsimd", taps["masks"], MASKS[:, :], "do", waits=[tok_masks])
    if "lgb" in taps:
        dma("gpsimd", taps["lgb"], LGB[:, :], "do", waits=[tok_lgb])
    if "ogfm" in taps:
        dma("gpsimd", taps["ogfm"], AL1[:, 0:8192], "do", waits=[tok_og_done])

    tok_x_attn = {}
    for c in range(NCH if (STOP_STAGE == 0 or STOP_STAGE >= 4) else 0):
        gates = [tok_gate_applied[(c, p)] for p in range(4)]
        for ot in range(KT):
            bank = GEN_BANKS[ot % 4]
            pw = [tok_wot] + gates + bank_free(bank)
            for kt4 in range(4):
                lhsT = W2REG[:, kt4 * 1024 + ot * 128: kt4 * 1024 + (ot + 1) * 128]
                rhs = AL1[:, kt4 * 2048 + c * CW: kt4 * 2048 + (c + 1) * CW]
                t_ = mm(PB[bank][:, 0:512], lhsT, rhs, kt4 == 0, kt4 == 3,
                        waits=pw if kt4 == 0 else [])
            etok = dve(lambda e, ot=ot, c=c, b=bank: e.tensor_add(
                X[:, ot * T + c * CW: ot * T + (c + 1) * CW],
                X[:, ot * T + c * CW: ot * T + (c + 1) * CW],
                PB[b][:, 0:512]), waits=[t_])
            set_bank(bank, etok)
            tok_x_attn[(ot, c)] = etok
    if "x1" in taps:
        dma("gpsimd", taps["x1"], X[:, :], "do", waits=[("sv", B.cnt["sv"])])

    FS = {"prev_ring_pe": None, "prev_piece_pe": {0: None, 1: None},
          "prev_stg_recip": None, "prev_wv_pe": None, "prev_hs_read": None,
          "tok_tapblock": None}
    prev_st1_read = {}
    prev_webf_read = {}
    tok_x_ln1 = {}
    tok_vfb = {}
    tok_x_vf = {}
    tok_wem = {}
    tok_hp = {}
    tok_x_delta = {}
    HP = lambda sl: AL1[:, sl]
    WEM = AL1[0:8, 4096:6144]
    HS = AL1[:, 6144:6656]
    PVSQ = AL1[0:64, 6656:7168]
    RC = AL1F[:, 4096:6144]
    tok_proj_dead = prev_chunk_attn_tok
    tok_wo_dead = ("sv", B.cnt["sv"])

    def ln_chunk(phase, c, out_dma=False):
        toks_norm = {}
        pm, pq = 2, 3
        for kt in range(KT):
            wc = [tok_x_attn[(kt, c)]] if phase == "ln1" else [tok_x_delta[(kt, c)]]
            if phase == "ln1":
                wc.append(prev_chunk_attn_tok)
            if FS["prev_ring_pe"] is not None:
                wc.append(FS["prev_ring_pe"])
            ctok = act(lambda e, kt=kt, c=c: e.copy(
                RING[:, kt * CW:(kt + 1) * CW],
                X[:, kt * T + c * CW: kt * T + (c + 1) * CW]), waits=wc)
            psl = slice((kt % 2) * 512, (kt % 2) * 512 + 512)
            wsq = [ctok]
            if FS["prev_piece_pe"][kt % 2] is not None:
                wsq.append(FS["prev_piece_pe"][kt % 2])
            sqtok = act(lambda e, kt=kt, psl=psl: e.activation(
                PIECES[:, psl], RING[:, kt * CW:(kt + 1) * CW], AF.Square),
                waits=wsq)
            w0 = []
            if kt == 0:
                w0 += bank_free(pm) + bank_free(pq)
            mm(PB[pm][0:1, 0:512], ONESC[:, :], RING[:, kt * CW:(kt + 1) * CW],
               kt == 0, kt == KT - 1, waits=w0 + [ctok])
            t_q = mm(PB[pq][0:1, 0:512], ONESC[:, :], PIECES[:, psl],
                     kt == 0, kt == KT - 1, waits=[sqtok])
            FS["prev_piece_pe"][kt % 2] = t_q
        FS["prev_ring_pe"] = t_q
        act(lambda e: e.activation(STRS[0:1, 0:512], PB[pm][0:1, 0:512],
                                   AF.Copy, scale=1.0 / D), waits=[t_q])
        act(lambda e: e.activation(STRB[0:1, 0:512], STRS[0:1, 0:512],
                                   AF.Copy, scale=-1.0))
        act(lambda e: e.activation(STRS[0:1, 0:512], STRS[0:1, 0:512],
                                   AF.Square))
        tok_msq = ("ss", B.cnt["ss"])
        dve(lambda e: e.scalar_tensor_tensor(
            STRS[0:1, 0:512], PB[pq][0:1, 0:512], 1.0 / D,
            STRS[0:1, 0:512], ALU.mult, ALU.subtract), waits=[tok_msq])
        tok_var = ("sv", B.cnt["sv"])
        act(lambda e: e.activation(STRS[0:1, 0:512], STRS[0:1, 0:512],
                                   AF.Sqrt, bias=CEPS[0:1, 0:1]), waits=[tok_var])
        tok_sq2 = ("ss", B.cnt["ss"])
        set_bank(pm, tok_sq2)
        set_bank(pq, tok_sq2)
        wcp = [tok_sq2, tok_stg_free]
        if FS["prev_stg_recip"] is not None:
            wcp.append(FS["prev_stg_recip"])
        cpt = act(lambda e: e.copy(STG[0:1, 0:512], STRS[0:1, 0:512]), waits=wcp)
        tok_a = dve(lambda e: e.reciprocal(STRS[0:1, 0:512], STG[0:1, 0:512]),
                    waits=[cpt])
        FS["prev_stg_recip"] = tok_a
        act(lambda e: e.activation(STRB[0:1, 512:1024], STRS[0:1, 0:512],
                                   AF.Copy), waits=[tok_a])
        tok_strb = ("ss", B.cnt["ss"])
        bn, ba = 4, 6
        wb = [tok_strb] + bank_free(bn) + bank_free(ba)
        if phase == "ln1" and c == 0:
            wb.append(prev_pb4_free)
        mm(PB[bn][0:128, 0:512], ONESR[0:1, :], STRB[0:1, 0:512], True, True,
           waits=wb)
        t_bc = mm(PB[ba][0:128, 0:512], ONESR[0:1, :], STRB[0:1, 512:1024],
                  True, True)
        last = None
        for kt in range(KT):
            dve(lambda e, kt=kt, c=c: e.tensor_add(
                X[:, kt * T + c * CW: kt * T + (c + 1) * CW],
                X[:, kt * T + c * CW: kt * T + (c + 1) * CW],
                PB[bn][:, 0:512]), waits=[t_bc])
            last = dve(lambda e, kt=kt, c=c: e.tensor_mul(
                X[:, kt * T + c * CW: kt * T + (c + 1) * CW],
                X[:, kt * T + c * CW: kt * T + (c + 1) * CW],
                PB[ba][:, 0:512]))
            toks_norm[(kt, c)] = last
            if out_dma:
                dma("gpsimd", outT[kt * 128:(kt + 1) * 128, c * CW:(c + 1) * CW],
                    X[:, kt * T + c * CW: kt * T + (c + 1) * CW], "do",
                    waits=[last])
        set_bank(bn, last)
        set_bank(ba, last)
        return toks_norm

    def wv_chunk(c):
        for kt in range(KT):
            wv_w = [tok_x_ln1[(kt, c)]]
            if FS["prev_ring_pe"] is not None:
                wv_w.append(FS["prev_ring_pe"])
            ctok = act(lambda e, kt=kt, c=c: e.copy(
                RING[:, kt * CW:(kt + 1) * CW],
                X[:, kt * T + c * CW: kt * T + (c + 1) * CW]),
                waits=wv_w)
            tok_ring[kt] = ctok
        for ot in range(KT):
            bank = GEN_BANKS[ot % 4]
            pw = [tok_wvt] + bank_free(bank)
            for kt in range(KT):
                lhsT = W3REG[:, kt * 1024 + ot * 128: kt * 1024 + (ot + 1) * 128]
                rhs = RING[:, kt * CW:(kt + 1) * CW]
                t_ = mm(PB[bank][:, 0:512], lhsT, rhs, kt == 0, kt == KT - 1,
                        waits=pw + [tok_ring[kt]] if kt == 0 else [tok_ring[kt]])
            vtok = act(lambda e, ot=ot, c=c, b=bank: e.activation(
                W1REG[:, ot * 2048 + c * CW: ot * 2048 + (c + 1) * CW],
                PB[b][:, 0:512], AF.Identity, bias=BVS[:, ot:ot + 1]),
                waits=[t_, tok_proj_dead, tok_small])
            xtok = dve(lambda e, ot=ot, c=c, b=bank: e.tensor_add(
                X[:, ot * T + c * CW: ot * T + (c + 1) * CW],
                X[:, ot * T + c * CW: ot * T + (c + 1) * CW],
                PB[b][:, 0:512]), waits=[t_, vtok])
            xtok2 = dve(lambda e, ot=ot, c=c: e.tensor_scalar_add(
                X[:, ot * T + c * CW: ot * T + (c + 1) * CW],
                X[:, ot * T + c * CW: ot * T + (c + 1) * CW],
                BVS[:, ot:ot + 1]))
            set_bank(bank, vtok, ("sv", B.cnt["sv"]))
            tok_vfb[(ot, c)] = vtok
            tok_x_vf[(ot, c)] = xtok2
        FS["prev_ring_pe"] = ("sp", B.cnt["sp"])

    def route_chunk(c):
        nonlocal prev_ptb_free
        tok_vfb_c = [tok_vfb[(ot, c)] for ot in range(KT)]
        bank = 2
        pw = [tok_wrp, tok_wo_dead] + tok_vfb_c + bank_free(bank)
        for kt in range(KT):
            lhsT = WRPB[:, kt * 72:(kt + 1) * 72]
            rhs = W1REG[:, kt * 2048 + c * CW: kt * 2048 + (c + 1) * CW]
            t_ = mm(PB[bank][0:72, 0:512], lhsT, rhs, kt == 0, kt == KT - 1,
                    waits=pw if kt == 0 else [])
        rtok = act(lambda e, c=c, b=bank: e.copy(
            RC[64:72, c * CW:(c + 1) * CW], PB[b][64:72, 0:512]), waits=[t_])
        sqtok = act(lambda e, b=bank: e.activation(
            PVSQ[:, 0:512], PB[b][0:64, 0:512], AF.Square), waits=[t_])
        set_bank(bank, ("ss", B.cnt["ss"]))
        wcm = [sqtok, tok_consts] + bank_free(3)
        t_c = mm(PB[3][0:8, 0:512], BLK[:, :], PVSQ[:, 0:512], True, True, waits=wcm)
        ctok = act(lambda e, c=c: e.activation(
            RC[0:8, c * CW:(c + 1) * CW], PB[3][0:8, 0:512], AF.Sqrt,
            scale=1.0 / 8.0), waits=[t_c])
        set_bank(3, ctok)

        for tt in range(4):
            g_tt = c * 4 + tt
            tcol = c * CW + tt * SW
            wt1 = [rtok, ctok, ("ss", B.cnt["ss"])] + bank_free(4) + bank_free(5)
            if prev_st1_read.get(g_tt % 2):
                wt1.append(prev_st1_read[g_tt % 2])
            tr(PB[5][0:128, 0:8], RC[64:72, tcol:tcol + 128], EYEF[64:72, 64:72],
               waits=wt1)
            t_tr = tr(PB[4][0:128, 0:8], RC[0:8, tcol:tcol + 128], EYEF[0:8, 0:8])
            sp_ = (g_tt % 2) * 16
            stok = act(lambda e, sp_=sp_: e.copy(ST1[:, sp_:sp_ + 8], PB[5][0:128, 0:8]),
                       waits=[t_tr])
            stok2 = act(lambda e, sp_=sp_: e.copy(ST1[:, sp_ + 8:sp_ + 16], PB[4][0:128, 0:8]))
            set_bank(4, stok2)
            set_bank(5, stok)
            w0 = [stok2, tok_grb, tok_consts]
            if FS["tok_tapblock"] is not None:
                w0.append(FS["tok_tapblock"])

            def fill(e):
                return e.tensor_copy(FILL[:, 0:1], GRB[:, 0:1])
            dve(lambda e, sp_=sp_: e.scalar_tensor_tensor(
                SRT[:, 0:8], ST1[:, sp_:sp_ + 8], 1.0, GRB[:, 24:32],
                ALU.mult, ALU.add, accum_out=SRT[:, 56:57]), waits=w0)
            dve(lambda e, sp_=sp_: e.tensor_scalar(
                SRT[:, 8:16], ST1[:, sp_ + 8:sp_ + 16], 0.0, 0.0, ALU.add,
                ALU.add, accum_out=SRT[:, 57:58]))
            prev_st1_read[g_tt % 2] = ("sv", B.cnt["sv"])
            dve(lambda e: e.tensor_scalar(SRT[:, 32:40], SRT[:, 0:8], 8.0,
                                          SRT[:, 56:57], ALU.mult, ALU.subtract))
            dve(lambda e: e.tensor_scalar(SRT[:, 40:48], SRT[:, 8:16], 8.0,
                                          SRT[:, 57:58], ALU.mult, ALU.subtract))
            tok_d8 = ("sv", B.cnt["sv"])
            act(lambda e: e.activation(SRT[:, 48:56], SRT[:, 32:40], AF.Square,
                                       accum_out=SRT[:, 58:59]), waits=[tok_d8])
            act(lambda e: e.activation(SRT[:, 48:56], SRT[:, 40:48], AF.Square,
                                       accum_out=SRT[:, 59:60]))
            act(lambda e: e.activation(SRT[:, 60:61], SRT[:, 58:59], AF.Sqrt,
                                       scale=0.125, bias=CEPS64[:, 0:1]))
            act(lambda e: e.activation(SRT[:, 61:62], SRT[:, 59:60], AF.Sqrt,
                                       scale=0.125, bias=CEPS64[:, 0:1]))
            tok_sq = ("ss", B.cnt["ss"])
            dve(lambda e: e.reciprocal(SRT[:, 62:63], SRT[:, 60:61]),
                waits=[tok_sq])
            dve(lambda e: e.reciprocal(SRT[:, 63:64], SRT[:, 61:62]))
            dve(lambda e: e.tensor_scalar_mul(SRT[:, 0:8], SRT[:, 32:40], SRT[:, 62:63]))
            dve(lambda e: e.tensor_scalar_mul(SRT[:, 8:16], SRT[:, 40:48], SRT[:, 63:64]))
            dve(lambda e: e.tensor_mul(SRT[:, 16:24], SRT[:, 0:8], GRB[:, 0:8]))
            dve(lambda e: e.tensor_mul(SRT[:, 24:32], SRT[:, 8:16], GRB[:, 8:16]))
            dve(lambda e: e.tensor_add(SRT[:, 32:40], SRT[:, 16:24], GRB[:, 16:24]))
            dve(fill)
            dve(lambda e: e.tensor_add(SRT[:, 40:48], SRT[:, 32:40], SRT[:, 24:32]))
            dve(fill)
            dve(lambda e: e.tensor_reduce(SRT[:, 56:57], SRT[:, 40:48], AX.X, ALU.max))
            dve(fill)
            dve(lambda e: e.tensor_scalar(SRT[:, 0:8], SRT[:, 40:48], SRT[:, 56:57],
                                          None, ALU.is_ge))
            dve(fill)
            dve(lambda e: e.scalar_tensor_tensor(SRT[:, 8:16], SRT[:, 0:8], -1e30,
                                                 SRT[:, 40:48], ALU.mult, ALU.add))
            dve(fill)
            dve(lambda e: e.tensor_reduce(SRT[:, 57:58], SRT[:, 8:16], AX.X, ALU.max))
            dve(lambda e: e.tensor_scalar_mul(SRT[:, 58:59], SRT[:, 56:57], -1.0))
            tok_nm1 = ("sv", B.cnt["sv"])
            act(lambda e: e.activation(SRT[:, 48:56], SRT[:, 40:48], AF.Exp,
                                       bias=SRT[:, 58:59]), waits=[tok_nm1])
            tok_es = ("ss", B.cnt["ss"])
            dve(lambda e: e.tensor_scalar(SRT[:, 16:24], SRT[:, 40:48], SRT[:, 57:58],
                                          None, ALU.is_ge), waits=[tok_es])
            dve(fill)
            dve(lambda e: e.scalar_tensor_tensor(SRT[:, 24:32], SRT[:, 48:56], 1.0,
                                                 SRT[:, 16:24], ALU.mult, ALU.mult,
                                                 accum_out=SRT[:, 59:60]))
            dve(fill)
            dve(lambda e: e.reciprocal(SRT[:, 60:61], SRT[:, 59:60]))
            dve(fill)
            wp_ = (g_tt % 2) * 8
            wprev = []
            if prev_webf_read.get(g_tt % 2):
                wprev.append(prev_webf_read[g_tt % 2])
            tok_we = dve(lambda e, wp_=wp_: e.tensor_scalar_mul(
                WEBF[:, wp_:wp_ + 8], SRT[:, 24:32], SRT[:, 60:61]), waits=wprev)
            wtr = [tok_we]
            if prev_ptb_free:
                wtr.append(prev_ptb_free)
            t_wt = tr(PTB[0:8, 0:128], WEBF[:, wp_:wp_ + 8], EYEB[:, :], waits=wtr)
            prev_webf_read[g_tt % 2] = t_wt
            if ("srt" in taps) and g_tt == 0:
                dma("gpsimd", taps["srt"], SRT[:, :], "do", waits=[tok_we])
                dma("gpsimd", taps["st1"], ST1[:, :], "do", waits=[tok_we])
                FS["tok_tapblock"] = ("do", B.cnt["do"])
            wemtok = act(lambda e, tcol=tcol: e.copy(
                WEM[0:8, tcol:tcol + 128], PTB[0:8, 0:128]), waits=[t_wt])
            prev_ptb_free = wemtok
        tok_wem[c] = ("ss", B.cnt["ss"])

    def w1_chunk(c):
        for ot in range(2):
            bank = 2 + ot
            pw = [tok_w1t] + [tok_vfb[(o_, c)] for o_ in range(KT)] + bank_free(bank)
            for kt in range(KT):
                lhsT = W4REG[:, kt * 256 + ot * 128: kt * 256 + (ot + 1) * 128]
                rhs = W1REG[:, kt * 2048 + c * CW: kt * 2048 + (c + 1) * CW]
                t_ = mm(PB[bank][:, 0:512], lhsT, rhs, kt == 0, kt == KT - 1,
                        waits=pw if kt == 0 else [])
            hw_ = [t_, tok_small]
            if FS["prev_hs_read"] is not None:
                hw_.append(FS["prev_hs_read"])
            htok = act(lambda e, b=bank, ot=ot: e.activation(
                HS[:, 0:512], PB[b][:, 0:512],
                AF.Silu, bias=B1CS[:, ot:ot + 1]), waits=hw_)
            wbank = 4 if ot == 0 else 6
            wwb = [tok_wem[c], tok_consts] + bank_free(wbank)
            t_wb = mm(PB[wbank][0:128, 0:512], ONEH[0:8, ot * 128:(ot + 1) * 128],
                      WEM[0:8, c * CW:(c + 1) * CW], True, True, waits=wwb)
            hptok = dve(lambda e, ot=ot, c=c, wb=wbank: e.tensor_mul(
                AL1[:, ot * 2048 + c * CW: ot * 2048 + (c + 1) * CW],
                HS[:, 0:512], PB[wb][:, 0:512]), waits=[htok, t_wb])
            FS["prev_hs_read"] = hptok
            set_bank(bank, htok)
            set_bank(wbank, hptok)
            tok_hp[(ot, c)] = hptok

    def w2_chunk(c):
        for ot in range(KT):
            bank = GEN_BANKS[ot % 4]
            pw = [tok_w2t, tok_b2, tok_hp[(0, c)], tok_hp[(1, c)], tok_wem[c]] + bank_free(bank)
            for kt2 in range(2):
                lhsT = W4REG[:, 2048 + kt2 * 1024 + ot * 128: 2048 + kt2 * 1024 + (ot + 1) * 128]
                rhs = AL1[:, kt2 * 2048 + c * CW: kt2 * 2048 + (c + 1) * CW]
                mm(PB[bank][:, 0:512], lhsT, rhs, kt2 == 0, False,
                   waits=pw if kt2 == 0 else [])
            t_ = mm(PB[bank][:, 0:512], B2B[0:8, ot * 128:(ot + 1) * 128],
                    WEM[0:8, c * CW:(c + 1) * CW], False, True)
            etok = dve(lambda e, ot=ot, c=c, b=bank: e.tensor_add(
                X[:, ot * T + c * CW: ot * T + (c + 1) * CW],
                X[:, ot * T + c * CW: ot * T + (c + 1) * CW],
                PB[b][:, 0:512]), waits=[t_, tok_x_vf[(ot, c)]])
            set_bank(bank, etok)
            tok_x_delta[(ot, c)] = etok

    STAGE_N = {5: 1, 6: 2, 7: 3, 8: 4, 9: 5}
    nsteps = 6 if STOP_STAGE == 0 else STAGE_N.get(STOP_STAGE, 0)
    for c in range(NCH if nsteps > 0 else 0):
        if nsteps >= 1:
            tok_x_ln1.update(ln_chunk("ln1", c))
        if nsteps >= 2:
            wv_chunk(c)
        if nsteps >= 3:
            route_chunk(c)
        if nsteps >= 4:
            w1_chunk(c)
        if nsteps >= 5:
            w2_chunk(c)
        if nsteps >= 6:
            ln_chunk("ln2", c, out_dma=True)

    tok_vfb_all = ("ss", B.cnt["ss"])
    if "x2" in taps:
        dma("gpsimd", taps["x2"], X[:, :], "do", waits=[("sv", B.cnt["sv"])])
    if "vfb" in taps:
        dma("gpsimd", taps["vfb"], W1REG[:, :], "do", waits=[tok_vfb_all])
    if "rc" in taps:
        dma("gpsimd", taps["rc"], RC[:, :], "do", waits=[("ss", B.cnt["ss"])])
    if "wem" in taps and NCH - 1 in tok_wem:
        dma("gpsimd", taps["wem"], WEM, "do", waits=[tok_wem[NCH - 1]])
    if "hp" in taps:
        dma("gpsimd", taps["hp"], AL1[:, 0:4096], "do", waits=[("sv", B.cnt["sv"])])
    if STOP_STAGE != 0:
        wlast = [("sv", B.cnt["sv"]), ("ss", B.cnt["ss"]),
                 ("sp", B.cnt["sp"]), ("sg", B.cnt["sg"])]
        for kt in range(KT):
            dma("gpsimd", outT[kt * 128:(kt + 1) * 128, :],
                X[:, kt * T:(kt + 1) * T], "do", waits=wlast if kt == 0 else [])
    if "x3" in taps:
        dma("gpsimd", taps["x3"], X[:, :], "do", waits=[("sv", B.cnt["sv"])])

    final_do = B.cnt["do"]
    B.emit("gpsimd", lambda e: e.nop(), waits=[("do", final_do)])

    with nc.Block() as blk:
        @blk.sync
        def _(e):
            for f in B.q["sync"]:
                f(e)

        @blk.tensor
        def _(e):
            for f in B.q["tensor"]:
                f(e)

        @blk.vector
        def _(e):
            for f in B.q["vector"]:
                f(e)

        @blk.scalar
        def _(e):
            for f in B.q["scalar"]:
                f(e)

        @blk.gpsimd
        def _(e):
            for f in B.q["gpsimd"]:
                f(e)

    return nc


def kernel(**inputs):
    x = inputs["L0_attn_x"]
    f32 = np.float32
    asc = np.ascontiguousarray

    shared = {
        "wqt": asc(np.asarray(inputs["L0_attn_W_Q"], f32).T),
        "wkt": asc(np.asarray(inputs["L0_attn_W_K"], f32).T),
        "wgt": asc(np.asarray(inputs["L0_attn_W_g_w"], f32).T),
        "wva": asc(np.asarray(inputs["L0_attn_W_V"], f32).T),
        "wot": asc(np.asarray(inputs["L0_attn_W_O"], f32).T),
        "wvt": asc(np.asarray(inputs["L0_ffn_Wv"], f32).T),
        "wrp": asc(np.concatenate([
            np.asarray(inputs["L0_ffn_Wp"], f32).reshape(64, 1024).T,
            np.asarray(inputs["L0_ffn_Wg"], f32).T], axis=1)),
        "w1t": asc(np.asarray(inputs["L0_ffn_W1"], f32).reshape(256, 1024).T),
        "w2t": asc(np.asarray(inputs["L0_ffn_W2"], f32).transpose(0, 2, 1).reshape(256, 1024)),
        "b2d": asc(np.asarray(inputs["L0_ffn_b2"], f32)),
        "bvT": asc(np.asarray(inputs["L0_ffn_bv"], f32).reshape(8, 128).T),
        "wgbT": asc(np.asarray(inputs["L0_attn_W_g_b"], f32).reshape(4, 128).T),
        "b1cT": asc(np.asarray(inputs["L0_ffn_b1"], f32).reshape(2, 128).T),
        "smallp": asc(np.concatenate([
            np.asarray(inputs["L0_ffn_g_r"], f32).ravel(),
            np.asarray(inputs["L0_ffn_b_r"], f32).ravel(),
            np.asarray(inputs["L0_ffn_g_c"], f32).ravel(),
            np.asarray(inputs["L0_ffn_b_c"], f32).ravel(),
            np.asarray(inputs["L0_attn_decay_log"], f32).ravel(),
            np.asarray(inputs["L0_ffn_alpha"], f32).ravel(),
            np.asarray(inputs["L0_ffn_bg"], f32).ravel(),
            np.zeros(15, f32)]).reshape(1, 64)),
    }

    nc = build_nc()
    in_maps = []
    for b in range(8):
        m = dict(shared)
        m["xt"] = asc(np.asarray(x[b], f32).T)
        in_maps.append(m)
    res = run_bass_kernel_spmd(nc, in_maps, core_ids=list(range(8)))
    out = np.stack([np.asarray(r["outT"]).T for r in res.results], axis=0)
    if DEBUG_TAPS:
        kernel.tap_results = res.results
    return np.ascontiguousarray(out.astype(np.float32))


# version 05
# speedup vs baseline: 1.1663x; 1.1663x over previous
import numpy as np
import concourse.bass as bass
import concourse.mybir as mybir
from concourse.bass_utils import run_bass_kernel_spmd

F32 = mybir.dt.float32
BF16 = mybir.dt.bfloat16
AF = mybir.ActivationFunctionType
ALU = mybir.AluOpType
AX = mybir.AxisListType

DEBUG_TAPS = ()
STOP_STAGE = 0
WVPART = 3

T = 2048
D = 1024
KT = 8
HD = 512
H = 8
DK = 64
NCH = 4
CW = 512
NSUB = 4
SW = 128
EPS_ATT = 1e-6
EPS_LN = 1e-5

ENGS = ("sync", "tensor", "vector", "scalar", "gpsimd")


class Builder:
    def __init__(self, nc):
        self.nc = nc
        self.q = {e: [] for e in ENGS}
        self.sems = {}
        self.cnt = {}
        self.last_wait = {e: {} for e in ENGS}

    def sem(self, name):
        if name not in self.sems:
            self.sems[name] = self.nc.alloc_semaphore(name)
            self.cnt[name] = 0
        return self.sems[name]

    def emit(self, eng, fn, waits=(), inc=None, inc_n=1):
        lw = self.last_wait[eng]
        actual = []
        for w in waits:
            if w is None:
                continue
            s, v = w
            if v <= 0 or lw.get(s, 0) >= v:
                continue
            lw[s] = v
            actual.append((s, v))
        tok = None
        if inc is not None:
            self.sem(inc)
            self.cnt[inc] += inc_n
            tok = (inc, self.cnt[inc])
        sems = self.sems

        def run(e, fn=fn, actual=actual, inc=inc, inc_n=inc_n):
            for s, v in actual:
                e.wait_ge(sems[s], v)
            inst = fn(e)
            if inc is not None:
                inst.then_inc(sems[inc], inc_n)
        self.q[eng].append(run)
        return tok


def build_nc():
    nc = bass.Bass("TRN2", target_bir_lowering=False, debug=False)

    xt = nc.dram_tensor("xt", [D, T], F32, kind="ExternalInput").ap()
    wqt = nc.dram_tensor("wqt", [D, HD], F32, kind="ExternalInput").ap()
    wkt = nc.dram_tensor("wkt", [D, HD], F32, kind="ExternalInput").ap()
    wgt = nc.dram_tensor("wgt", [D, HD], F32, kind="ExternalInput").ap()
    wva = nc.dram_tensor("wva", [D, HD], F32, kind="ExternalInput").ap()
    wot = nc.dram_tensor("wot", [HD, D], F32, kind="ExternalInput").ap()
    wvt = nc.dram_tensor("wvt", [D, D], F32, kind="ExternalInput").ap()
    wrp = nc.dram_tensor("wrp", [D, 72], F32, kind="ExternalInput").ap()
    w1t = nc.dram_tensor("w1t", [D, 256], F32, kind="ExternalInput").ap()
    w2t = nc.dram_tensor("w2t", [256, D], F32, kind="ExternalInput").ap()
    b2d = nc.dram_tensor("b2d", [8, D], F32, kind="ExternalInput").ap()
    bvT = nc.dram_tensor("bvT", [128, 8], F32, kind="ExternalInput").ap()
    wgbT = nc.dram_tensor("wgbT", [128, 4], F32, kind="ExternalInput").ap()
    b1cT = nc.dram_tensor("b1cT", [128, 2], F32, kind="ExternalInput").ap()
    smallp = nc.dram_tensor("smallp", [1, 64], F32, kind="ExternalInput").ap()
    outT = nc.dram_tensor("outT", [D, T], F32, kind="ExternalOutput").ap()

    taps = {}
    tapspec = {
        "qf": [128, 2048], "kf": [128, 2048], "gt": [128, 2048],
        "vt": [128, 2080], "at0": [128, 512], "sall": [128, 260],
        "ogfm": [128, 8192], "x1": [128, 16384], "x2": [128, 16384],
        "vfb": [128, 16384], "rc": [128, 2048], "wem": [8, 2048],
        "hp": [128, 4096], "x3": [128, 16384], "qde0": [128, 256],
        "srt": [128, 64], "st1": [128, 32], "grb": [128, 32], "small": [1, 64], "srt2": [128, 64],
        "masks": [128, 1024], "lgb": [128, 8],
    }
    for t_ in DEBUG_TAPS:
        taps[t_] = nc.dram_tensor("tap_" + t_, tapspec[t_], F32 if t_ in
                                  ("sall", "x1", "x2", "rc", "x3", "masks", "lgb", "srt", "st1", "grb", "small", "srt2")
                                  else BF16, kind="ExternalOutput").ap()

    A = nc.alloc_sbuf_tensor
    X = A("X", [128, KT * T], F32)
    RING = A("RING", [128, KT * CW], BF16)
    STG = A("STG", [128, 1024], F32)
    W1REG = A("W1REG", [128, 16384], BF16)
    W2REG = A("W2REG", [128, 4096], BF16)
    W3REG = A("W3REG", [128, 8192], BF16)
    W4REG = A("W4REG", [128, 4096], BF16)
    WRPB = A("WRPB", [128, 576], BF16)
    B2B = A("B2B", [8, 1024], BF16)
    B2S = A("B2S", [8, 256], F32)
    AL1 = A("AL1", [128, 12288], BF16)
    AL1F = AL1.bitcast(F32)
    GT = A("GTb", [128, 2048], BF16)
    VT = A("VTb", [128, 2080], BF16)
    OGB = A("OGB", [64, 2048], BF16)
    ATS = A("ATS", [128, 1024], BF16)
    KDB = A("KDB", [128, 256], BF16)
    QDE = A("QDE", [128, 512], BF16)
    RDB = A("RDBb", [64, 512], BF16)
    SALL = A("SALL", [128, 260], F32)
    SBF = A("SBFb", [128, 520], BF16)
    E1 = A("E1", [128, 512], BF16)
    R1 = A("R1", [128, 512], BF16)
    PIECES = A("PIECES", [128, 1024], BF16)
    MASKS = A("MASKS", [128, 1024], F32)
    QD = A("QD", [128, 512], F32)
    GD2 = A("GD2", [128, 512], F32)
    IO = A("IO", [128, 128], F32)
    TRI = A("TRI", [128, 128], BF16)
    IOD = A("IOD", [128, 64], BF16)
    EYEF = A("EYEF", [128, 128], F32)
    EYEB = A("EYEB", [128, 128], BF16)
    LGB = A("LGB", [128, 8], F32)
    LGP = A("LGP", [128, 4], F32)
    GCP = A("GCP", [128, 4], F32)
    STRS = A("STRS", [1, 512], F32)
    STRB = A("STRB", [1, 1024], BF16)
    ONESC = A("ONESC", [128, 1], BF16)
    ONESR = A("ONESR", [1, 128], BF16)
    ONESRF = A("ONESRF", [1, 128], F32)
    ONEH = A("ONEH", [8, 256], BF16)
    OH44 = A("OH44", [4, 256], BF16)
    RDH = A("RDH", [4, 128], BF16)
    BLK = A("BLK", [64, 8], BF16)
    BLKF = A("BLKF", [64, 8], F32)
    SMALL = A("SMALLs", [1, 64], F32)
    LGROW = A("LGROW", [1, 32], F32)
    BVS = A("BVS", [128, 8], F32)
    WGBS = A("WGBS", [128, 4], F32)
    B1CS = A("B1CS", [128, 2], F32)
    GRB = A("GRB", [128, 32], F32)
    CEPS = A("CEPS", [128, 1], F32)
    CEPS64 = A("CEPS64", [128, 1], F32)
    FILL = A("FILL", [128, 1], F32)
    ST1 = A("ST1", [128, 32], F32)
    SRT = A("SRT", [128, 64], F32)
    WEBF = A("WEBF", [128, 16], BF16)

    P = nc.alloc_psum_tensor
    PB = [P("pb%d" % i, [128, 512], F32) for i in range(7)]
    PTB = P("ptb", [128, 1024], BF16)

    B = Builder(nc)
    for s in ("dx", "dwa", "dwb", "dga", "dgb", "dt", "do",
              "sp", "sv", "ss", "sg"):
        B.sem(s)

    def mm(out_ap, lhsT, rhs, start, stop, waits=()):
        return B.emit("tensor",
                      lambda e, o=out_ap, l=lhsT, r=rhs, st=start, sp_=stop:
                      e.matmul(o, l, r, start=st, stop=sp_),
                      waits=waits, inc="sp")

    def tr(out_ap, in_ap, ident, waits=()):
        return B.emit("tensor",
                      lambda e, o=out_ap, i=in_ap, d=ident:
                      e.transpose(o, i, d),
                      waits=waits, inc="sp")

    def dve(fn, waits=()):
        return B.emit("vector", fn, waits=waits, inc="sv")

    def act(fn, waits=()):
        return B.emit("scalar", fn, waits=waits, inc="ss")

    def gp(fn, waits=()):
        return B.emit("gpsimd", fn, waits=waits, inc="sg")

    def dma(queue, out_ap, in_ap, sem, waits=()):
        return B.emit(queue,
                      lambda e, o=out_ap, i=in_ap: e.dma_start(out=o, in_=i),
                      waits=waits, inc=sem, inc_n=16)

    dma("gpsimd", SMALL[:, :], smallp, "dga")
    dma("gpsimd", BVS[:, :], bvT, "dga")
    dma("gpsimd", WGBS[:, :], wgbT, "dga")
    dma("gpsimd", B1CS[:, :], b1cT, "dga")
    tok_small = ("dga", B.cnt["dga"])
    for kt in range(KT):
        q = "gpsimd" if kt % 2 == 0 else "scalar"
        dma(q, X[:, kt * T:(kt + 1) * T], xt[kt * 128:(kt + 1) * 128, :], "dx")
    tok_x_all = ("dx", B.cnt["dx"])

    gp(lambda e: e.iota(IO[:, :], [[1, 128]], channel_multiplier=-1,
                        allow_small_or_imprecise_dtypes=True))
    gp(lambda e: e.iota(IOD[:, :], [[0, 64]], channel_multiplier=-1, base=127,
                        allow_small_or_imprecise_dtypes=True))
    gp(lambda e: e.tensor_single_scalar(TRI[:, :], IO[:, :], 0.0, ALU.is_ge))
    gp(lambda e: e.tensor_single_scalar(EYEF[:, :], IO[:, :], 0.0, ALU.is_equal))
    gp(lambda e: e.tensor_copy(EYEB[:, :], EYEF[:, :]))
    gp(lambda e: e.memset(ONESC[:, :], 1.0))
    gp(lambda e: e.memset(CEPS[:, :], EPS_LN))
    gp(lambda e: e.memset(CEPS64[:, :], 64.0 * EPS_LN))
    gp(lambda e: e.memset(ONESR[:, :], 1.0))
    gp(lambda e: e.memset(ONESRF[:, :], 1.0))
    gp(lambda e: e.memset(VT[:, :], 1.0))
    gp(lambda e: e.memset(SALL[:, :], 0.0))
    gp(lambda e: e.memset(SBF[:, :], 0.0))
    gp(lambda e: e.iota(BLKF[:, :], [[-8, 8]], channel_multiplier=1,
                        allow_small_or_imprecise_dtypes=True))
    gp(lambda e: e.tensor_single_scalar(BLK[:, :], BLKF[:, :], 7.0, ALU.is_le))
    gp(lambda e: e.tensor_single_scalar(BLKF[:, :], BLKF[:, :], 0.0, ALU.is_ge))
    gp(lambda e: e.tensor_mul(BLK[:, :], BLK[:, :], BLKF[:, :]))
    for ot in range(2):
        gp(lambda e, ot=ot: e.iota(ONEH[0:8, ot * 128:(ot + 1) * 128],
                                   [[1, 4], [0, 32]], channel_multiplier=-1,
                                   base=4 * ot,
                                   allow_small_or_imprecise_dtypes=True))
    gp(lambda e: e.tensor_single_scalar(ONEH[0:8, :], ONEH[0:8, :], 0.0, ALU.is_equal))
    gp(lambda e: e.iota(OH44[0:4, :], [[1, 4], [0, 64]], channel_multiplier=-1,
                        allow_small_or_imprecise_dtypes=True))
    gp(lambda e: e.tensor_single_scalar(OH44[0:4, :], OH44[0:4, :], 0.0, ALU.is_equal))
    tok_consts = ("sg", B.cnt["sg"])

    for hh in range(4):
        dma("gpsimd", B2S[0:8, 0:256], b2d[:, hh * 256:(hh + 1) * 256], "dga",
            waits=[("sg", B.cnt["sg"])] if hh else [])
        tok_b2dma = ("dga", B.cnt["dga"])
        gp(lambda e, hh=hh: e.tensor_copy(B2B[:, hh * 256:(hh + 1) * 256],
                                          B2S[0:8, 0:256]), waits=[tok_b2dma])
    tok_b2 = ("sg", B.cnt["sg"])

    act(lambda e: e.activation(LGROW[0:1, 0:8], SMALL[0:1, 32:40], AF.Sigmoid),
        waits=[tok_small])
    dve(lambda e: e.tensor_scalar_max(LGROW[0:1, 0:8], LGROW[0:1, 0:8], 1e-30),
        waits=[("ss", B.cnt["ss"])])
    act(lambda e: e.activation(LGROW[0:1, 8:16], LGROW[0:1, 0:8], AF.Ln),
        waits=[("sv", B.cnt["sv"])])
    tok_lg = ("ss", B.cnt["ss"])
    mm(PB[6][0:128, 0:8], ONESRF[0:1, :], LGROW[0:1, 8:16], True, True,
       waits=[tok_lg, tok_consts])
    act(lambda e: e.copy(LGB[:, :], PB[6][0:128, 0:8]),
        waits=[("sp", B.cnt["sp"])])
    tok_lgb = ("ss", B.cnt["ss"])
    for p in range(4):
        gp(lambda e, p=p: e.tensor_copy(LGP[0:64, p:p + 1], LGB[0:64, 2 * p:2 * p + 1]),
           waits=[tok_lgb])
        gp(lambda e, p=p: e.tensor_copy(LGP[64:128, p:p + 1],
                                        LGB[64:128, 2 * p + 1:2 * p + 2]))
    tok_lgp = ("sg", B.cnt["sg"])
    act(lambda e: e.activation(GCP[:, :], LGP[:, :], AF.Exp, scale=128.0),
        waits=[tok_lgp])
    tok_gcp = ("ss", B.cnt["ss"])
    def mask_col(h):
        return (h // 4) * 512 + (h % 2) * 256 + ((h % 4) // 2) * 128
    for h in range(H):
        sl = slice(mask_col(h), mask_col(h) + 128)
        dve(lambda e, sl=sl, h=h: e.tensor_scalar(MASKS[:, sl], IO[:, :],
                                                   LGB[:, h:h + 1], 0.0,
                                                   ALU.mult, ALU.min),
            waits=[tok_lgb, tok_consts])
        act(lambda e, sl=sl: e.activation(MASKS[:, sl], MASKS[:, sl], AF.Exp),
            waits=[("sv", B.cnt["sv"])])
        dve(lambda e, sl=sl: e.tensor_mul(MASKS[:, sl], MASKS[:, sl], TRI[:, :]),
            waits=[("ss", B.cnt["ss"])])
    tok_masks = ("sv", B.cnt["sv"])
    for p in range(4):
        sl = slice(p * 128, (p + 1) * 128)
        gp(lambda e, sl=sl: e.iota(QD[:, sl], [[1, 128]], channel_multiplier=0,
                                   base=1, allow_small_or_imprecise_dtypes=True),
           waits=[tok_lgp])
        dve(lambda e, sl=sl, p=p: e.tensor_scalar_mul(QD[:, sl], QD[:, sl],
                                                      LGP[:, p:p + 1]),
            waits=[("sg", B.cnt["sg"])])
        act(lambda e, sl=sl: e.activation(QD[:, sl], QD[:, sl], AF.Exp),
            waits=[("sv", B.cnt["sv"])])
        for half in range(2):
            h = 2 * p + half
            s2 = slice(p * 128 + half * 64, p * 128 + (half + 1) * 64)
            dve(lambda e, s2=s2, h=h: e.tensor_tensor(
                GD2[:, s2], IOD[:, :], LGB[:, h:h + 1].broadcast_to([128, 64]),
                ALU.mult), waits=[tok_lgb, tok_consts])
        act(lambda e, sl=sl: e.activation(GD2[:, sl], GD2[:, sl], AF.Exp),
            waits=[("sv", B.cnt["sv"])])
    tok_qd = ("ss", B.cnt["ss"])

    act(lambda e: e.activation(LGROW[0:1, 30:31], SMALL[0:1, 40:41], AF.Sigmoid),
        waits=[tok_small])
    tok_sa = ("ss", B.cnt["ss"])
    dve(lambda e: e.tensor_scalar_mul(LGROW[0:1, 16:24], SMALL[0:1, 16:24],
                                      LGROW[0:1, 30:31]), waits=[tok_sa])
    dve(lambda e: e.scalar_tensor_tensor(LGROW[0:1, 24:32], SMALL[0:1, 24:32],
                                         LGROW[0:1, 30:31], SMALL[0:1, 8:16],
                                         ALU.mult, ALU.add))
    tok_rows = ("sv", B.cnt["sv"])
    mm(PB[6][0:128, 32:40], ONESRF[0:1, :], SMALL[0:1, 0:8], True, False,
       waits=[tok_rows, ("ss", B.cnt["ss"])])
    mm(PB[6][0:128, 40:48], ONESRF[0:1, :], LGROW[0:1, 16:24], False, False)
    mm(PB[6][0:128, 48:56], ONESRF[0:1, :], LGROW[0:1, 24:32], False, False)
    mm(PB[6][0:128, 56:64], ONESRF[0:1, :], SMALL[0:1, 41:49], False, True)
    act(lambda e: e.copy(GRB[:, :], PB[6][0:128, 32:64]),
        waits=[("sp", B.cnt["sp"])])
    tok_grb = ("ss", B.cnt["ss"])
    sc_slot_toks = {6: [tok_grb]}

    W3F = W3REG.bitcast(F32)
    prev_w3f_cast = None

    def load_big(dram_ap, nrows, cols, dest, dest_off, eng="scalar"):
        nonlocal prev_w3f_cast
        nk = nrows // 128
        src = dram_ap.rearrange("(k p) c -> p k c", p=128)
        w_ = [prev_w3f_cast] if prev_w3f_cast else []
        dma("sync", W3F[:, 0:nk * cols], src, "dwa", waits=w_)
        load_weight_sync.cn["dwa"] += 1
        dtok = ("dwa", load_weight_sync.cn["dwa"] * 16)
        if eng == "scalar":
            ct = act(lambda e, n=nk * cols, o=dest_off: e.copy(
                dest[:, o:o + n], W3F[:, 0:n]), waits=[dtok])
        else:
            ct = gp(lambda e, n=nk * cols, o=dest_off: e.tensor_copy(
                dest[:, o:o + n], W3F[:, 0:n]), waits=[dtok])
        prev_w3f_cast = ct
        return ct

    def load_weight_sync(name, dram_ap, n_rows, cols, dest, dest_off, cast_eng):
        nk = n_rows // 128
        toks = []
        for k in range(nk):
            parity = load_weight_sync.par
            load_weight_sync.par ^= 1
            slot = slice(parity * 512, parity * 512 + cols)
            semn = "dwa" if parity == 0 else "dwb"
            load_weight_sync.cn[semn] += 1
            prev = load_weight_sync.prev[parity]
            dma("sync", STG[:, slot], dram_ap[k * 128:(k + 1) * 128, :], semn,
                waits=[prev] if prev else [])
            dtok = (semn, load_weight_sync.cn[semn] * 16)
            dst = dest[:, dest_off + k * cols: dest_off + (k + 1) * cols]
            if cast_eng == "scalar":
                ct = act(lambda e, d=dst, s=STG[:, slot]: e.copy(d, s), waits=[dtok])
            else:
                ct = gp(lambda e, d=dst, s=STG[:, slot]: e.tensor_copy(d, s), waits=[dtok])
            load_weight_sync.prev[parity] = ct
            toks.append(ct)
        return toks[-1]
    load_weight_sync.par = 0
    load_weight_sync.prev = {0: None, 1: None}
    load_weight_sync.cn = {"dwa": 0, "dwb": 0}


    def load_halves_sync(dram_ap, n_rows, rowcols, dest, dest_off, extra_wait=None,
                         eng="scalar"):
        nk = n_rows // 128
        nh = rowcols // 512
        for k in range(nk):
            for hh in range(nh):
                parity = load_weight_sync.par
                load_weight_sync.par ^= 1
                slot = slice(parity * 512, parity * 512 + 512)
                semn = "dwa" if parity == 0 else "dwb"
                load_weight_sync.cn[semn] += 1
                prev = load_weight_sync.prev[parity]
                dma("sync", STG[:, slot],
                    dram_ap[k * 128:(k + 1) * 128, hh * 512:(hh + 1) * 512], semn,
                    waits=[prev] if prev else [])
                dtok = (semn, load_weight_sync.cn[semn] * 16)
                dst = dest[:, dest_off + k * rowcols + hh * 512:
                           dest_off + k * rowcols + (hh + 1) * 512]
                cw = [dtok]
                if extra_wait is not None:
                    cw.append(extra_wait)
                if eng == "scalar":
                    ct = act(lambda e, d=dst, s=STG[:, slot]: e.copy(d, s), waits=cw)
                else:
                    ct = gp(lambda e, d=dst, s=STG[:, slot]: e.tensor_copy(d, s), waits=cw)
                load_weight_sync.prev[parity] = ct
        return ct




    tok_wq = load_big(wqt, D, 512, W1REG, 0)
    tok_wk = load_big(wkt, D, 512, W1REG, 4096)
    tok_wg = load_big(wgt, D, 512, W1REG, 8192)
    tok_wva = load_big(wva, D, 512, W1REG, 12288)
    tok_wot = tok_w1t = tok_w2t = tok_wrp = tok_wvt = None
    tok_stg_free = None

    def OGfm(sl): return AL1[:, sl]
    QFB = lambda c0, c1: AL1[:, 8192 + c0: 8192 + c1]
    KFB = lambda c0, c1: AL1[:, 10240 + c0: 10240 + c1]

    GEN_BANKS = [0, 2, 3, 6]
    tok_ring = {}
    tok_gate_applied = {}
    tok_ogb_free = {}
    prev_chunk_attn_tok = None
    prev_qf_consumed = None
    prev_gt_consumed = None
    prev_vt_consumed = None

    def bank_free(b):
        return list(sc_slot_toks.get(b, []))

    def set_bank(b, *toks):
        sc_slot_toks[b] = [t for t in toks if t is not None]
    prev_e1_read = {0: None, 1: None}
    prev_at_read = {0: None, 1: None}
    prev_kd_read = {0: None, 1: None}
    prev_qde_read = {0: None, 1: None}
    prev_rdb_read = {0: None}
    prev_rdh_read = {0: None}
    prev_pb6_free = {0: None}
    prev_pb4_rd = {0: None, 1: None}
    prev_webf_tr = {0: None, 1: None}
    prev_sco_evac = {0: None}
    prev_sce_evac = {0: None}
    prev_o_evac = {0: None, 1: None}
    prev_sbf_tok = {0: tok_consts, 1: tok_consts}
    prev_pb4_free = None
    prev_ptb_free = None
    pend_wave = [None]
    tok_sall = tok_consts

    NCH_RUN = NCH if (STOP_STAGE == 0 or STOP_STAGE >= 3) else (1 if STOP_STAGE == 2 else 0)
    for c in range(NCH_RUN):
        for kt in range(KT):
            w = [tok_x_all]
            if prev_chunk_attn_tok is not None:
                w.append(prev_chunk_attn_tok)
            tok_ring[kt] = gp(lambda e, kt=kt, c=c: e.tensor_copy(
                RING[:, kt * CW:(kt + 1) * CW],
                X[:, kt * T + c * CW: kt * T + (c + 1) * CW]), waits=w)

        for (proj, wtok, woff, dest) in (("q", tok_wq, 0, "QF"), ("k", tok_wk, 4096, "KF")):
            for ot in range(4):
                bank = GEN_BANKS[ot]
                pw = [wtok, tok_ring[7]] + bank_free(bank)
                for kt in range(KT):
                    lhsT = W1REG[:, woff + kt * 512 + ot * 128: woff + kt * 512 + (ot + 1) * 128]
                    rhs = RING[:, kt * CW:(kt + 1) * CW]
                    t_ = mm(PB[bank][:, 0:512], lhsT, rhs, kt == 0, kt == KT - 1,
                            waits=pw + [tok_ring[kt]] if kt == 0 else [tok_ring[kt]])
                mmtok = t_
                ep = 0
                esl = slice(0, 512)
                wE = [mmtok]
                if prev_e1_read[ep]:
                    wE.append(prev_e1_read[ep])
                act(lambda e, esl=esl, b=bank: e.activation(E1[:, esl], PB[b][:, 0:512], AF.Exp),
                    waits=wE)
                act(lambda e, esl=esl, b=bank: e.activation(R1[:, esl], PB[b][:, 0:512], AF.Relu))
                tok_evac = ("ss", B.cnt["ss"])
                dsl = slice(ot * 512, (ot + 1) * 512)
                dst = QFB(dsl.start, dsl.stop) if dest == "QF" else KFB(dsl.start, dsl.stop)
                dw_ = [tok_evac]
                if prev_qf_consumed is not None:
                    dw_.append(prev_qf_consumed)
                stok = dve(lambda e, d=dst, esl=esl: e.scalar_tensor_tensor(
                    d, E1[:, esl], 1.0, R1[:, esl], ALU.min, ALU.add), waits=dw_)
                prev_e1_read[ep] = stok
                set_bank(bank, tok_evac, stok)

        for ot in range(4):
            bank = GEN_BANKS[ot]
            pw = [tok_wg] + bank_free(bank)
            for kt in range(KT):
                lhsT = W1REG[:, 8192 + kt * 512 + ot * 128: 8192 + kt * 512 + (ot + 1) * 128]
                rhs = RING[:, kt * CW:(kt + 1) * CW]
                t_ = mm(PB[bank][:, 0:512], lhsT, rhs, kt == 0, kt == KT - 1,
                        waits=pw if kt == 0 else [])
            gw = [t_]
            if prev_gt_consumed is not None:
                gw.append(prev_gt_consumed)
            gtok = act(lambda e, ot=ot, b=bank: e.activation(
                GT[:, ot * 512:(ot + 1) * 512], PB[b][:, 0:512], AF.Sigmoid,
                bias=WGBS[:, ot:ot + 1]), waits=gw + [tok_small])
            set_bank(bank, gtok)

        for sc_ in range(NSUB):
            bank = GEN_BANKS[sc_]
            pw = [tok_wva] + bank_free(bank)
            for kt in range(KT):
                lhsT = RING[:, kt * CW + sc_ * SW: kt * CW + (sc_ + 1) * SW]
                rhs = W1REG[:, 12288 + kt * 512: 12288 + (kt + 1) * 512]
                t_ = mm(PB[bank][:, 0:512], lhsT, rhs, kt == 0, kt == KT - 1,
                        waits=pw if kt == 0 else [])
            vw = [t_]
            if prev_vt_consumed is not None:
                vw.append(prev_vt_consumed)
            out_ap = bass.AP(VT, sc_ * 520, [[VT.shape[1], 128], [130, 4], [65, 2], [1, 64]])
            in_ap = bass.AP(PB[bank], 0, [[512, 128], [128, 4], [64, 2], [1, 64]])
            vtok = dve(lambda e, o=out_ap, i=in_ap: e.tensor_copy(o, i), waits=vw)
            set_bank(bank, vtok)
        tok_vt_ready = ("sv", B.cnt["sv"])
        tok_qkg_ready = tok_vt_ready
        tok_gt_ready = ("ss", B.cnt["ss"])

        def att_front(sc_, w):
            ctx = {"sc_": sc_, "w": w, "c": c}
            wse = [tok_qkg_ready, tok_masks] 
            if prev_sce_evac[0]:
                wse.append(prev_sce_evac[0])
            wso = []
            if prev_sco_evac[0]:
                wso.append(prev_sco_evac[0])
            for idx in range(2):
                p = 2 * w + idx
                qsl = slice(p * 512 + sc_ * SW, p * 512 + (sc_ + 1) * SW)
                mm(PB[0][:, idx * 128:(idx + 1) * 128],
                   KFB(qsl.start, qsl.stop)[0:64, :],
                   QFB(qsl.start, qsl.stop)[0:64, :], idx == 0, idx == 1,
                   waits=wse if idx == 0 else [])
                tok_sc_o = mm(PB[1][:, idx * 128:(idx + 1) * 128],
                              KFB(qsl.start, qsl.stop)[64:128, :],
                              QFB(qsl.start, qsl.stop)[64:128, :], idx == 0, idx == 1,
                              waits=wso if idx == 0 else [])
            wat = [tok_sc_o]
            if prev_at_read[w]:
                wat.append(prev_at_read[w])
            dve(lambda e, w=w: e.tensor_mul(
                ATS[:, w * 512: w * 512 + 256], PB[0][:, 0:256],
                MASKS[:, w * 512: w * 512 + 256]), waits=wat)
            tok_at_e = ("sv", B.cnt["sv"])
            dve(lambda e, w=w: e.tensor_mul(
                ATS[:, w * 512 + 256: w * 512 + 512], PB[1][:, 0:256],
                MASKS[:, w * 512 + 256: w * 512 + 512]))
            tok_at = ("sv", B.cnt["sv"])
            prev_sce_evac[0] = tok_at_e
            prev_sco_evac[0] = tok_at
            wqd = [tok_qkg_ready, tok_qd]
            if prev_qde_read[w]:
                wqd.append(prev_qde_read[w])
            q_in = bass.AP(AL1, 8192 + 2 * w * 512 + sc_ * SW,
                           [[AL1.shape[1], 128], [512, 2], [1, 128]])
            tok_qde = dve(lambda e, w=w, q_in=q_in: e.tensor_mul(
                QDE[:, w * 256:(w + 1) * 256], q_in,
                QD[:, 2 * w * 128: 2 * w * 128 + 256]), waits=wqd)
            ctx["tok_at"] = tok_at
            ctx["tok_qde"] = tok_qde
            return ctx

        def att_o(ctx):
            sc_, w = ctx["sc_"], ctx["w"]
            for idx in range(2):
                p = 2 * w + idx
                ob = 2 + idx
                wo = [ctx["tok_at"], ctx["tok_qde"], prev_sbf_tok[w], tok_vt_ready]
                if prev_o_evac[idx]:
                    wo.append(prev_o_evac[idx])
                mm(PB[ob][0:64, 0:128], SBF[:, p * 130: p * 130 + 64],
                   QDE[:, w * 256 + idx * 128: w * 256 + (idx + 1) * 128],
                   True, False, waits=wo)
                mm(PB[ob][0:64, 0:128],
                   VT[:, sc_ * 520 + p * 130: sc_ * 520 + p * 130 + 64],
                   ATS[:, w * 512 + idx * 128: w * 512 + (idx + 1) * 128],
                   False, False)
                mm(PB[ob][0:64, 128:256], SBF[:, p * 130 + 65: p * 130 + 129],
                   QDE[:, w * 256 + idx * 128: w * 256 + (idx + 1) * 128],
                   False, False)
                ctx["tok_o%d" % idx] = mm(
                    PB[ob][0:64, 128:256],
                    VT[:, sc_ * 520 + p * 130 + 65: sc_ * 520 + p * 130 + 129],
                    ATS[:, w * 512 + 256 + idx * 128: w * 512 + 256 + (idx + 1) * 128],
                    False, True)

        def att_back(ctx):
            nonlocal prev_ptb_free, tok_sall
            sc_, w, cc = ctx["sc_"], ctx["w"], ctx["c"]
            first = True
            wden = [ctx["tok_at"], ctx["tok_qde"], prev_sbf_tok[w]]
            if prev_pb4_rd[w]:
                wden.append(prev_pb4_rd[w])
            for idx in range(2):
                p = 2 * w + idx
                for side in range(2):
                    h = 4 * w + 2 * idx + side
                    atc = w * 512 + side * 256 + idx * 128
                    mm(PB[4][0:128, h:h + 1], ATS[:, atc:atc + 128], ONESC[:, :],
                       first, False, waits=wden if first else [])
                    first = False
                    zc = p * 130 + 64 + side * 65
                    tok_den = mm(PB[4][0:128, h:h + 1],
                                 QDE[:, w * 256 + idx * 128: w * 256 + (idx + 1) * 128],
                                 SBF[:, zc:zc + 1], False,
                                 (idx == 1 and side == 1))
            wdt = [tok_den]
            if prev_webf_tr[w]:
                wdt.append(prev_webf_tr[w])
            dve(lambda e, w=w: e.tensor_scalar_add(
                SRT[:, w * 4:(w + 1) * 4], PB[4][:, 4 * w: 4 * w + 4], EPS_ATT),
                waits=wdt)
            tok_dt = ("sv", B.cnt["sv"])
            prev_pb4_rd[w] = tok_dt
            dve(lambda e: e.tensor_copy(FILL[:, 0:1], GRB[:, 0:1]))

            def _recip(e, w=w):
                with nc.allow_low_precision(reason="bf16 rd is a matmul operand"):
                    return e.reciprocal(WEBF[:, w * 4:(w + 1) * 4], SRT[:, w * 4:(w + 1) * 4])
            tok_rd = dve(_recip)
            wtr = [tok_rd]
            if prev_ptb_free:
                wtr.append(prev_ptb_free)
            t_tr = tr(PTB[0:4, 0:128], WEBF[:, w * 4:(w + 1) * 4], EYEB[:, :],
                      waits=wtr)
            tok_rdh = act(lambda e: e.copy(RDH[0:4, :], PTB[0:4, 0:128]),
                          waits=[t_tr, prev_rdh_read[0]] if prev_rdh_read[0]
                          else [t_tr])
            prev_ptb_free = tok_rdh
            wbc = [tok_rdh, tok_consts]
            if prev_pb6_free[0]:
                wbc.append(prev_pb6_free[0])
            for hl in range(4):
                tok_bc = mm(PB[6][0:64, hl * 128:(hl + 1) * 128],
                            OH44[0:4, hl * 64:(hl + 1) * 64], RDH[0:4, :],
                            hl == 0, hl == 3, waits=wbc if hl == 0 else [])
            prev_rdh_read[0] = tok_bc
            wrb = [tok_bc]
            if prev_rdb_read[0]:
                wrb.append(prev_rdb_read[0])
            tok_rdb = dve(lambda e: e.tensor_copy(RDB[0:64, 0:512], PB[6][0:64, 0:512]),
                          waits=wrb)
            for idx in range(2):
                p = 2 * w + idx
                ob = 2 + idx
                og_col = p * 2048 + cc * CW + sc_ * SW
                dve(lambda e, ob=ob, og_col=og_col, hl=2 * idx: e.tensor_mul(
                    AL1[0:64, og_col:og_col + 128], PB[ob][0:64, 0:128],
                    RDB[0:64, hl * 128:(hl + 1) * 128]),
                    waits=[tok_rdb, ctx["tok_o%d" % idx]])
                ogb_col = p * 512 + sc_ * 128
                wog = []
                if tok_ogb_free.get(p):
                    wog.append(tok_ogb_free[p])
                tok_og = dve(lambda e, ob=ob, ogb_col=ogb_col, hl=2 * idx + 1: e.tensor_mul(
                    OGB[0:64, ogb_col:ogb_col + 128], PB[ob][0:64, 128:256],
                    RDB[0:64, hl * 128:(hl + 1) * 128]), waits=wog)
                prev_o_evac[idx] = tok_og
            prev_rdb_read[0] = tok_og
            wu0 = [tok_rdb]
            for idx in range(2):
                p = 2 * w + idx
                qsl = slice(p * 512 + sc_ * SW, p * 512 + (sc_ + 1) * SW)
                wtr2 = [prev_ptb_free]
                t_k = tr(PTB[:, 0:128], KFB(qsl.start, qsl.stop), EYEB[:, :],
                         waits=wtr2)
                ksl = slice(idx * 128, (idx + 1) * 128)
                wkd = [t_k, tok_qd]
                if prev_kd_read[idx]:
                    wkd.append(prev_kd_read[idx])
                tok_kd = dve(lambda e, ksl=ksl, p=p: e.tensor_mul(
                    KDB[:, ksl], PTB[:, 0:128], GD2[:, p * 128:(p + 1) * 128]),
                    waits=wkd)
                prev_ptb_free = tok_kd
                tok_u = mm(PB[6][0:128, idx * 130:(idx + 1) * 130], KDB[:, ksl],
                           VT[:, sc_ * 520 + p * 130: sc_ * 520 + (p + 1) * 130],
                           idx == 0, idx == 1,
                           waits=[tok_kd] + wu0 if idx == 0 else [tok_kd])
                prev_kd_read[idx] = tok_u
            for idx in range(2):
                p = 2 * w + idx
                for side in range(2):
                    rows = slice(side * 64, (side + 1) * 64)
                    ucol = idx * 130 + side * 65
                    dve(lambda e, p=p, rows=rows, ucol=ucol: e.scalar_tensor_tensor(
                        SALL[rows, p * 65:(p + 1) * 65], SALL[rows, p * 65:(p + 1) * 65],
                        GCP[rows, p:p + 1], PB[6][rows, ucol:ucol + 65],
                        ALU.mult, ALU.add),
                        waits=[tok_u, tok_gcp, tok_sall])
            tok_sall = ("sv", B.cnt["sv"])
            prev_pb6_free[0] = tok_sall
            for idx in range(2):
                p = 2 * w + idx
                act(lambda e, p=p: e.copy(SBF[0:64, p * 130: p * 130 + 65],
                                          SALL[0:64, p * 65:(p + 1) * 65]),
                    waits=[tok_sall])
                act(lambda e, p=p: e.copy(SBF[64:128, p * 130 + 65: p * 130 + 130],
                                          SALL[64:128, p * 65:(p + 1) * 65]))
            prev_sbf_tok[w] = ("ss", B.cnt["ss"])
            prev_at_read[w] = tok_den
            prev_qde_read[w] = tok_den

        for sc_ in range(NSUB):
            for w in range(2):
                ctxn = att_front(sc_, w)
                if pend_wave[0] is not None:
                    att_back(pend_wave[0])
                att_o(ctxn)
                pend_wave[0] = ctxn
        if pend_wave[0] is not None:
            att_back(pend_wave[0])
            pend_wave[0] = None

        if c == 0:
            tok_wot = load_big(wot, HD, D, W2REG, 0, eng="gpsimd")
            tok_w1t = load_big(w1t, D, 256, W4REG, 0, eng="gpsimd")
            tok_w2t = load_big(w2t, 256, D, W4REG, 2048, eng="gpsimd")
            tok_wrp = load_big(wrp, D, 72, WRPB, 0, eng="gpsimd")
        elif c == 1:
            tok_wvt = load_halves_sync(wvt, D, D, W3REG, 0,
                                       extra_wait=prev_w3f_cast, eng="gpsimd")
            tok_stg_free = ("sg", B.cnt["sg"])

        for p in range(4):
            in_ap = OGB[0:64, p * 512:(p + 1) * 512]
            out_ap = AL1[64:128, p * 2048 + c * CW: p * 2048 + (c + 1) * CW]
            dma("gpsimd", out_ap, in_ap, "dt", waits=[prev_o_evac[p % 2], tok_sall])
            tok_ogb_free[p] = ("dt", B.cnt["dt"])

        tok_stitch = ("dt", B.cnt["dt"])
        for p in range(4):
            tok_gate_applied[(c, p)] = dve(
                lambda e, p=p, c=c: e.tensor_mul(
                    AL1[:, p * 2048 + c * CW: p * 2048 + (c + 1) * CW],
                    AL1[:, p * 2048 + c * CW: p * 2048 + (c + 1) * CW],
                    GT[:, p * 512:(p + 1) * 512]),
                waits=[tok_stitch, tok_gt_ready])
        prev_gt_consumed = ("sv", B.cnt["sv"])
        prev_chunk_attn_tok = ("sp", B.cnt["sp"])
        prev_qf_consumed = prev_chunk_attn_tok
        prev_vt_consumed = prev_chunk_attn_tok
        set_bank(0, prev_sce_evac[0])
        set_bank(1, prev_sco_evac[0])
        set_bank(2, prev_o_evac[0])
        set_bank(3, prev_o_evac[1])
        set_bank(4, prev_pb4_rd[0], prev_pb4_rd[1])
        set_bank(6, prev_pb6_free[0])

        if "qf" in taps and c == 0:
            dma("gpsimd", taps["qf"], AL1[:, 8192:10240], "do", waits=[tok_qkg_ready])
        if "kf" in taps and c == 0:
            dma("gpsimd", taps["kf"], AL1[:, 10240:12288], "do", waits=[tok_qkg_ready])
        if "gt" in taps and c == 0:
            dma("gpsimd", taps["gt"], GT[:, :], "do", waits=[tok_gt_ready])
        if "vt" in taps and c == 0:
            dma("gpsimd", taps["vt"], VT[:, :], "do", waits=[tok_vt_ready])
        if "sall" in taps and c == 0:
            dma("gpsimd", taps["sall"], SALL[:, :], "do", waits=[tok_sall])

    tok_og_done = ("sv", B.cnt["sv"])
    if "grb" in taps:
        dma("gpsimd", taps["grb"], GRB[:, :], "do", waits=[tok_grb])
        dma("gpsimd", taps["small"], SMALL[:, :], "do", waits=[tok_grb])
    if "masks" in taps:
        dma("gpsimd", taps["masks"], MASKS[:, :], "do", waits=[tok_masks])
    if "lgb" in taps:
        dma("gpsimd", taps["lgb"], LGB[:, :], "do", waits=[tok_lgb])
    if "ogfm" in taps:
        dma("gpsimd", taps["ogfm"], AL1[:, 0:8192], "do", waits=[tok_og_done])

    tok_x_attn = {}
    for c in range(NCH if (STOP_STAGE == 0 or STOP_STAGE >= 4) else 0):
        gates = [tok_gate_applied[(c, p)] for p in range(4)]
        for ot in range(KT):
            bank = GEN_BANKS[ot % 4]
            pw = [tok_wot] + gates + bank_free(bank)
            for kt4 in range(4):
                lhsT = W2REG[:, kt4 * 1024 + ot * 128: kt4 * 1024 + (ot + 1) * 128]
                rhs = AL1[:, kt4 * 2048 + c * CW: kt4 * 2048 + (c + 1) * CW]
                t_ = mm(PB[bank][:, 0:512], lhsT, rhs, kt4 == 0, kt4 == 3,
                        waits=pw if kt4 == 0 else [])
            etok = dve(lambda e, ot=ot, c=c, b=bank: e.tensor_add(
                X[:, ot * T + c * CW: ot * T + (c + 1) * CW],
                X[:, ot * T + c * CW: ot * T + (c + 1) * CW],
                PB[b][:, 0:512]), waits=[t_])
            set_bank(bank, etok)
            tok_x_attn[(ot, c)] = etok
    if "x1" in taps:
        dma("gpsimd", taps["x1"], X[:, :], "do", waits=[("sv", B.cnt["sv"])])

    FS = {"prev_ring_pe": None, "prev_piece_pe": {0: None, 1: None},
          "prev_stg_recip": None, "prev_wv_pe": None, "prev_hs_read": None,
          "tok_tapblock": None}
    prev_st1_read = {}
    prev_webf_read = {}
    tok_x_ln1 = {}
    tok_vfb = {}
    tok_x_vf = {}
    tok_wem = {}
    tok_hp = {}
    tok_x_delta = {}
    HP = lambda sl: AL1[:, sl]
    WEM = AL1[0:8, 4096:6144]
    HS = AL1[:, 6144:6656]
    PVSQ = AL1[0:64, 6656:7168]
    RC = AL1F[:, 4096:6144]
    tok_proj_dead = prev_chunk_attn_tok
    tok_wo_dead = ("sv", B.cnt["sv"])

    def ln_chunk(phase, c, out_dma=False):
        toks_norm = {}
        pm, pq = 2, 3
        for kt in range(KT):
            wc = [tok_x_attn[(kt, c)]] if phase == "ln1" else [tok_x_delta[(kt, c)]]
            if phase == "ln1":
                wc.append(prev_chunk_attn_tok)
            if FS["prev_ring_pe"] is not None:
                wc.append(FS["prev_ring_pe"])
            ctok = act(lambda e, kt=kt, c=c: e.copy(
                RING[:, kt * CW:(kt + 1) * CW],
                X[:, kt * T + c * CW: kt * T + (c + 1) * CW]), waits=wc)
            psl = slice((kt % 2) * 512, (kt % 2) * 512 + 512)
            wsq = [ctok]
            if FS["prev_piece_pe"][kt % 2] is not None:
                wsq.append(FS["prev_piece_pe"][kt % 2])
            sqtok = act(lambda e, kt=kt, psl=psl: e.activation(
                PIECES[:, psl], RING[:, kt * CW:(kt + 1) * CW], AF.Square),
                waits=wsq)
            w0 = []
            if kt == 0:
                w0 += bank_free(pm) + bank_free(pq)
            mm(PB[pm][0:1, 0:512], ONESC[:, :], RING[:, kt * CW:(kt + 1) * CW],
               kt == 0, kt == KT - 1, waits=w0 + [ctok])
            t_q = mm(PB[pq][0:1, 0:512], ONESC[:, :], PIECES[:, psl],
                     kt == 0, kt == KT - 1, waits=[sqtok])
            FS["prev_piece_pe"][kt % 2] = t_q
        FS["prev_ring_pe"] = t_q
        act(lambda e: e.activation(STRS[0:1, 0:512], PB[pm][0:1, 0:512],
                                   AF.Copy, scale=1.0 / D), waits=[t_q])
        act(lambda e: e.activation(STRB[0:1, 0:512], STRS[0:1, 0:512],
                                   AF.Copy, scale=-1.0))
        act(lambda e: e.activation(STRS[0:1, 0:512], STRS[0:1, 0:512],
                                   AF.Square))
        tok_msq = ("ss", B.cnt["ss"])
        dve(lambda e: e.scalar_tensor_tensor(
            STRS[0:1, 0:512], PB[pq][0:1, 0:512], 1.0 / D,
            STRS[0:1, 0:512], ALU.mult, ALU.subtract), waits=[tok_msq])
        tok_var = ("sv", B.cnt["sv"])
        act(lambda e: e.activation(STRS[0:1, 0:512], STRS[0:1, 0:512],
                                   AF.Sqrt, bias=CEPS[0:1, 0:1]), waits=[tok_var])
        tok_sq2 = ("ss", B.cnt["ss"])
        set_bank(pm, tok_sq2)
        set_bank(pq, tok_sq2)
        wcp = [tok_sq2] + ([tok_stg_free] if tok_stg_free else [])
        if FS["prev_stg_recip"] is not None:
            wcp.append(FS["prev_stg_recip"])
        cpt = act(lambda e: e.copy(STG[0:1, 0:512], STRS[0:1, 0:512]), waits=wcp)
        tok_a = dve(lambda e: e.reciprocal(STRS[0:1, 0:512], STG[0:1, 0:512]),
                    waits=[cpt])
        FS["prev_stg_recip"] = tok_a
        act(lambda e: e.activation(STRB[0:1, 512:1024], STRS[0:1, 0:512],
                                   AF.Copy), waits=[tok_a])
        tok_strb = ("ss", B.cnt["ss"])
        bn, ba = 4, 6
        wb = [tok_strb] + bank_free(bn) + bank_free(ba)
        if phase == "ln1" and c == 0:
            wb.append(prev_pb4_free)
        mm(PB[bn][0:128, 0:512], ONESR[0:1, :], STRB[0:1, 0:512], True, True,
           waits=wb)
        t_bc = mm(PB[ba][0:128, 0:512], ONESR[0:1, :], STRB[0:1, 512:1024],
                  True, True)
        last = None
        for kt in range(KT):
            dve(lambda e, kt=kt, c=c: e.tensor_add(
                X[:, kt * T + c * CW: kt * T + (c + 1) * CW],
                X[:, kt * T + c * CW: kt * T + (c + 1) * CW],
                PB[bn][:, 0:512]), waits=[t_bc])
            last = dve(lambda e, kt=kt, c=c: e.tensor_mul(
                X[:, kt * T + c * CW: kt * T + (c + 1) * CW],
                X[:, kt * T + c * CW: kt * T + (c + 1) * CW],
                PB[ba][:, 0:512]))
            toks_norm[(kt, c)] = last
            if out_dma:
                dma("gpsimd", outT[kt * 128:(kt + 1) * 128, c * CW:(c + 1) * CW],
                    X[:, kt * T + c * CW: kt * T + (c + 1) * CW], "do",
                    waits=[last])
        set_bank(bn, last)
        set_bank(ba, last)
        return toks_norm

    def wv_chunk(c):
        for kt in range(KT):
            wv_w = [tok_x_ln1[(kt, c)]]
            if FS["prev_ring_pe"] is not None:
                wv_w.append(FS["prev_ring_pe"])
            ctok = act(lambda e, kt=kt, c=c: e.copy(
                RING[:, kt * CW:(kt + 1) * CW],
                X[:, kt * T + c * CW: kt * T + (c + 1) * CW]),
                waits=wv_w)
            tok_ring[kt] = ctok
        for ot in range(KT):
            bank = GEN_BANKS[ot % 4]
            pw = [tok_wvt] + bank_free(bank)
            for kt in range(KT):
                lhsT = W3REG[:, kt * 1024 + ot * 128: kt * 1024 + (ot + 1) * 128]
                rhs = RING[:, kt * CW:(kt + 1) * CW]
                t_ = mm(PB[bank][:, 0:512], lhsT, rhs, kt == 0, kt == KT - 1,
                        waits=pw + [tok_ring[kt]] if kt == 0 else [tok_ring[kt]])
            vtok = act(lambda e, ot=ot, c=c, b=bank: e.activation(
                W1REG[:, ot * 2048 + c * CW: ot * 2048 + (c + 1) * CW],
                PB[b][:, 0:512], AF.Identity, bias=BVS[:, ot:ot + 1]),
                waits=[t_, tok_proj_dead, tok_small])
            xtok = dve(lambda e, ot=ot, c=c, b=bank: e.tensor_add(
                X[:, ot * T + c * CW: ot * T + (c + 1) * CW],
                X[:, ot * T + c * CW: ot * T + (c + 1) * CW],
                PB[b][:, 0:512]), waits=[t_, vtok])
            xtok2 = dve(lambda e, ot=ot, c=c: e.tensor_scalar_add(
                X[:, ot * T + c * CW: ot * T + (c + 1) * CW],
                X[:, ot * T + c * CW: ot * T + (c + 1) * CW],
                BVS[:, ot:ot + 1]))
            set_bank(bank, vtok, ("sv", B.cnt["sv"]))
            tok_vfb[(ot, c)] = vtok
            tok_x_vf[(ot, c)] = xtok2
        FS["prev_ring_pe"] = ("sp", B.cnt["sp"])

    def route_chunk(c):
        nonlocal prev_ptb_free
        tok_vfb_c = [tok_vfb[(ot, c)] for ot in range(KT)]
        bank = 2
        pw = [tok_wrp, tok_wo_dead] + tok_vfb_c + bank_free(bank)
        for kt in range(KT):
            lhsT = WRPB[:, kt * 72:(kt + 1) * 72]
            rhs = W1REG[:, kt * 2048 + c * CW: kt * 2048 + (c + 1) * CW]
            t_ = mm(PB[bank][0:72, 0:512], lhsT, rhs, kt == 0, kt == KT - 1,
                    waits=pw if kt == 0 else [])
        rtok = act(lambda e, c=c, b=bank: e.copy(
            RC[64:72, c * CW:(c + 1) * CW], PB[b][64:72, 0:512]), waits=[t_])
        sqtok = act(lambda e, b=bank: e.activation(
            PVSQ[:, 0:512], PB[b][0:64, 0:512], AF.Square), waits=[t_])
        set_bank(bank, ("ss", B.cnt["ss"]))
        wcm = [sqtok, tok_consts] + bank_free(3)
        t_c = mm(PB[3][0:8, 0:512], BLK[:, :], PVSQ[:, 0:512], True, True, waits=wcm)
        ctok = act(lambda e, c=c: e.activation(
            RC[0:8, c * CW:(c + 1) * CW], PB[3][0:8, 0:512], AF.Sqrt,
            scale=1.0 / 8.0), waits=[t_c])
        set_bank(3, ctok)

        for tt in range(4):
            g_tt = c * 4 + tt
            tcol = c * CW + tt * SW
            wt1 = [rtok, ctok, ("ss", B.cnt["ss"])] + bank_free(4) + bank_free(5)
            if prev_st1_read.get(g_tt % 2):
                wt1.append(prev_st1_read[g_tt % 2])
            tr(PB[5][0:128, 0:8], RC[64:72, tcol:tcol + 128], EYEF[64:72, 64:72],
               waits=wt1)
            t_tr = tr(PB[4][0:128, 0:8], RC[0:8, tcol:tcol + 128], EYEF[0:8, 0:8])
            sp_ = (g_tt % 2) * 16
            stok = act(lambda e, sp_=sp_: e.copy(ST1[:, sp_:sp_ + 8], PB[5][0:128, 0:8]),
                       waits=[t_tr])
            stok2 = act(lambda e, sp_=sp_: e.copy(ST1[:, sp_ + 8:sp_ + 16], PB[4][0:128, 0:8]))
            set_bank(4, stok2)
            set_bank(5, stok)
            w0 = [stok2, tok_grb, tok_consts]
            if FS["tok_tapblock"] is not None:
                w0.append(FS["tok_tapblock"])

            def fill(e):
                return e.tensor_copy(FILL[:, 0:1], GRB[:, 0:1])
            dve(lambda e, sp_=sp_: e.scalar_tensor_tensor(
                SRT[:, 0:8], ST1[:, sp_:sp_ + 8], 1.0, GRB[:, 24:32],
                ALU.mult, ALU.add, accum_out=SRT[:, 56:57]), waits=w0)
            dve(lambda e, sp_=sp_: e.tensor_scalar(
                SRT[:, 8:16], ST1[:, sp_ + 8:sp_ + 16], 0.0, 0.0, ALU.add,
                ALU.add, accum_out=SRT[:, 57:58]))
            prev_st1_read[g_tt % 2] = ("sv", B.cnt["sv"])
            dve(lambda e: e.tensor_scalar(SRT[:, 32:40], SRT[:, 0:8], 8.0,
                                          SRT[:, 56:57], ALU.mult, ALU.subtract))
            dve(lambda e: e.tensor_scalar(SRT[:, 40:48], SRT[:, 8:16], 8.0,
                                          SRT[:, 57:58], ALU.mult, ALU.subtract))
            tok_d8 = ("sv", B.cnt["sv"])
            act(lambda e: e.activation(SRT[:, 48:56], SRT[:, 32:40], AF.Square,
                                       accum_out=SRT[:, 58:59]), waits=[tok_d8])
            act(lambda e: e.activation(SRT[:, 48:56], SRT[:, 40:48], AF.Square,
                                       accum_out=SRT[:, 59:60]))
            act(lambda e: e.activation(SRT[:, 60:61], SRT[:, 58:59], AF.Sqrt,
                                       scale=0.125, bias=CEPS64[:, 0:1]))
            act(lambda e: e.activation(SRT[:, 61:62], SRT[:, 59:60], AF.Sqrt,
                                       scale=0.125, bias=CEPS64[:, 0:1]))
            tok_sq = ("ss", B.cnt["ss"])
            dve(lambda e: e.reciprocal(SRT[:, 62:63], SRT[:, 60:61]),
                waits=[tok_sq])
            dve(lambda e: e.reciprocal(SRT[:, 63:64], SRT[:, 61:62]))
            dve(lambda e: e.tensor_scalar_mul(SRT[:, 0:8], SRT[:, 32:40], SRT[:, 62:63]))
            dve(lambda e: e.tensor_scalar_mul(SRT[:, 8:16], SRT[:, 40:48], SRT[:, 63:64]))
            dve(lambda e: e.tensor_mul(SRT[:, 16:24], SRT[:, 0:8], GRB[:, 0:8]))
            dve(lambda e: e.tensor_mul(SRT[:, 24:32], SRT[:, 8:16], GRB[:, 8:16]))
            dve(lambda e: e.tensor_add(SRT[:, 32:40], SRT[:, 16:24], GRB[:, 16:24]))
            dve(fill)
            dve(lambda e: e.tensor_add(SRT[:, 40:48], SRT[:, 32:40], SRT[:, 24:32]))
            dve(fill)
            dve(lambda e: e.tensor_reduce(SRT[:, 56:57], SRT[:, 40:48], AX.X, ALU.max))
            dve(fill)
            dve(lambda e: e.tensor_scalar(SRT[:, 0:8], SRT[:, 40:48], SRT[:, 56:57],
                                          None, ALU.is_ge))
            dve(fill)
            dve(lambda e: e.scalar_tensor_tensor(SRT[:, 8:16], SRT[:, 0:8], -1e30,
                                                 SRT[:, 40:48], ALU.mult, ALU.add))
            dve(fill)
            dve(lambda e: e.tensor_reduce(SRT[:, 57:58], SRT[:, 8:16], AX.X, ALU.max))
            dve(lambda e: e.tensor_scalar_mul(SRT[:, 58:59], SRT[:, 56:57], -1.0))
            tok_nm1 = ("sv", B.cnt["sv"])
            act(lambda e: e.activation(SRT[:, 48:56], SRT[:, 40:48], AF.Exp,
                                       bias=SRT[:, 58:59]), waits=[tok_nm1])
            tok_es = ("ss", B.cnt["ss"])
            dve(lambda e: e.tensor_scalar(SRT[:, 16:24], SRT[:, 40:48], SRT[:, 57:58],
                                          None, ALU.is_ge), waits=[tok_es])
            dve(fill)
            dve(lambda e: e.scalar_tensor_tensor(SRT[:, 24:32], SRT[:, 48:56], 1.0,
                                                 SRT[:, 16:24], ALU.mult, ALU.mult,
                                                 accum_out=SRT[:, 59:60]))
            dve(fill)
            dve(lambda e: e.reciprocal(SRT[:, 60:61], SRT[:, 59:60]))
            dve(fill)
            wp_ = (g_tt % 2) * 8
            wprev = []
            if prev_webf_read.get(g_tt % 2):
                wprev.append(prev_webf_read[g_tt % 2])
            tok_we = dve(lambda e, wp_=wp_: e.tensor_scalar_mul(
                WEBF[:, wp_:wp_ + 8], SRT[:, 24:32], SRT[:, 60:61]), waits=wprev)
            wtr = [tok_we]
            if prev_ptb_free:
                wtr.append(prev_ptb_free)
            t_wt = tr(PTB[0:8, 0:128], WEBF[:, wp_:wp_ + 8], EYEB[:, :], waits=wtr)
            prev_webf_read[g_tt % 2] = t_wt
            if ("srt" in taps) and g_tt == 0:
                dma("gpsimd", taps["srt"], SRT[:, :], "do", waits=[tok_we])
                dma("gpsimd", taps["st1"], ST1[:, :], "do", waits=[tok_we])
                FS["tok_tapblock"] = ("do", B.cnt["do"])
            wemtok = act(lambda e, tcol=tcol: e.copy(
                WEM[0:8, tcol:tcol + 128], PTB[0:8, 0:128]), waits=[t_wt])
            prev_ptb_free = wemtok
        tok_wem[c] = ("ss", B.cnt["ss"])

    def w1_chunk(c):
        for ot in range(2):
            bank = 2 + ot
            pw = [tok_w1t] + [tok_vfb[(o_, c)] for o_ in range(KT)] + bank_free(bank)
            for kt in range(KT):
                lhsT = W4REG[:, kt * 256 + ot * 128: kt * 256 + (ot + 1) * 128]
                rhs = W1REG[:, kt * 2048 + c * CW: kt * 2048 + (c + 1) * CW]
                t_ = mm(PB[bank][:, 0:512], lhsT, rhs, kt == 0, kt == KT - 1,
                        waits=pw if kt == 0 else [])
            hw_ = [t_, tok_small]
            if FS["prev_hs_read"] is not None:
                hw_.append(FS["prev_hs_read"])
            htok = act(lambda e, b=bank, ot=ot: e.activation(
                HS[:, 0:512], PB[b][:, 0:512],
                AF.Silu, bias=B1CS[:, ot:ot + 1]), waits=hw_)
            wbank = 4 if ot == 0 else 6
            wwb = [tok_wem[c], tok_consts] + bank_free(wbank)
            t_wb = mm(PB[wbank][0:128, 0:512], ONEH[0:8, ot * 128:(ot + 1) * 128],
                      WEM[0:8, c * CW:(c + 1) * CW], True, True, waits=wwb)
            hptok = dve(lambda e, ot=ot, c=c, wb=wbank: e.tensor_mul(
                AL1[:, ot * 2048 + c * CW: ot * 2048 + (c + 1) * CW],
                HS[:, 0:512], PB[wb][:, 0:512]), waits=[htok, t_wb])
            FS["prev_hs_read"] = hptok
            set_bank(bank, htok)
            set_bank(wbank, hptok)
            tok_hp[(ot, c)] = hptok

    def w2_chunk(c):
        for ot in range(KT):
            bank = GEN_BANKS[ot % 4]
            pw = [tok_w2t, tok_b2, tok_hp[(0, c)], tok_hp[(1, c)], tok_wem[c]] + bank_free(bank)
            for kt2 in range(2):
                lhsT = W4REG[:, 2048 + kt2 * 1024 + ot * 128: 2048 + kt2 * 1024 + (ot + 1) * 128]
                rhs = AL1[:, kt2 * 2048 + c * CW: kt2 * 2048 + (c + 1) * CW]
                mm(PB[bank][:, 0:512], lhsT, rhs, kt2 == 0, False,
                   waits=pw if kt2 == 0 else [])
            t_ = mm(PB[bank][:, 0:512], B2B[0:8, ot * 128:(ot + 1) * 128],
                    WEM[0:8, c * CW:(c + 1) * CW], False, True)
            etok = dve(lambda e, ot=ot, c=c, b=bank: e.tensor_add(
                X[:, ot * T + c * CW: ot * T + (c + 1) * CW],
                X[:, ot * T + c * CW: ot * T + (c + 1) * CW],
                PB[b][:, 0:512]), waits=[t_, tok_x_vf[(ot, c)]])
            set_bank(bank, etok)
            tok_x_delta[(ot, c)] = etok

    STAGE_N = {5: 1, 6: 2, 7: 3, 8: 4, 9: 5}
    nsteps = 6 if STOP_STAGE == 0 else STAGE_N.get(STOP_STAGE, 0)
    for c in range(NCH if nsteps > 0 else 0):
        if nsteps >= 1:
            tok_x_ln1.update(ln_chunk("ln1", c))
        if nsteps >= 2:
            wv_chunk(c)
        if nsteps >= 3:
            route_chunk(c)
        if nsteps >= 4:
            w1_chunk(c)
        if nsteps >= 5:
            w2_chunk(c)
        if nsteps >= 6:
            ln_chunk("ln2", c, out_dma=True)

    tok_vfb_all = ("ss", B.cnt["ss"])
    if "x2" in taps:
        dma("gpsimd", taps["x2"], X[:, :], "do", waits=[("sv", B.cnt["sv"])])
    if "vfb" in taps:
        dma("gpsimd", taps["vfb"], W1REG[:, :], "do", waits=[tok_vfb_all])
    if "rc" in taps:
        dma("gpsimd", taps["rc"], RC[:, :], "do", waits=[("ss", B.cnt["ss"])])
    if "wem" in taps and NCH - 1 in tok_wem:
        dma("gpsimd", taps["wem"], WEM, "do", waits=[tok_wem[NCH - 1]])
    if "hp" in taps:
        dma("gpsimd", taps["hp"], AL1[:, 0:4096], "do", waits=[("sv", B.cnt["sv"])])
    if STOP_STAGE != 0:
        wlast = [("sv", B.cnt["sv"]), ("ss", B.cnt["ss"]),
                 ("sp", B.cnt["sp"]), ("sg", B.cnt["sg"])]
        for kt in range(KT):
            dma("gpsimd", outT[kt * 128:(kt + 1) * 128, :],
                X[:, kt * T:(kt + 1) * T], "do", waits=wlast if kt == 0 else [])
    if "x3" in taps:
        dma("gpsimd", taps["x3"], X[:, :], "do", waits=[("sv", B.cnt["sv"])])

    final_do = B.cnt["do"]
    B.emit("gpsimd", lambda e: e.nop(), waits=[("do", final_do)])

    with nc.Block() as blk:
        @blk.sync
        def _(e):
            for f in B.q["sync"]:
                f(e)

        @blk.tensor
        def _(e):
            for f in B.q["tensor"]:
                f(e)

        @blk.vector
        def _(e):
            for f in B.q["vector"]:
                f(e)

        @blk.scalar
        def _(e):
            for f in B.q["scalar"]:
                f(e)

        @blk.gpsimd
        def _(e):
            for f in B.q["gpsimd"]:
                f(e)

    return nc


def kernel(**inputs):
    x = inputs["L0_attn_x"]
    f32 = np.float32
    asc = np.ascontiguousarray

    shared = {
        "wqt": asc(np.asarray(inputs["L0_attn_W_Q"], f32).T),
        "wkt": asc(np.asarray(inputs["L0_attn_W_K"], f32).T),
        "wgt": asc(np.asarray(inputs["L0_attn_W_g_w"], f32).T),
        "wva": asc(np.asarray(inputs["L0_attn_W_V"], f32).T),
        "wot": asc(np.asarray(inputs["L0_attn_W_O"], f32).T),
        "wvt": asc(np.asarray(inputs["L0_ffn_Wv"], f32).T),
        "wrp": asc(np.concatenate([
            np.asarray(inputs["L0_ffn_Wp"], f32).reshape(64, 1024).T,
            np.asarray(inputs["L0_ffn_Wg"], f32).T], axis=1)),
        "w1t": asc(np.asarray(inputs["L0_ffn_W1"], f32).reshape(256, 1024).T),
        "w2t": asc(np.asarray(inputs["L0_ffn_W2"], f32).transpose(0, 2, 1).reshape(256, 1024)),
        "b2d": asc(np.asarray(inputs["L0_ffn_b2"], f32)),
        "bvT": asc(np.asarray(inputs["L0_ffn_bv"], f32).reshape(8, 128).T),
        "wgbT": asc(np.asarray(inputs["L0_attn_W_g_b"], f32).reshape(4, 128).T),
        "b1cT": asc(np.asarray(inputs["L0_ffn_b1"], f32).reshape(2, 128).T),
        "smallp": asc(np.concatenate([
            np.asarray(inputs["L0_ffn_g_r"], f32).ravel(),
            np.asarray(inputs["L0_ffn_b_r"], f32).ravel(),
            np.asarray(inputs["L0_ffn_g_c"], f32).ravel(),
            np.asarray(inputs["L0_ffn_b_c"], f32).ravel(),
            np.asarray(inputs["L0_attn_decay_log"], f32).ravel(),
            np.asarray(inputs["L0_ffn_alpha"], f32).ravel(),
            np.asarray(inputs["L0_ffn_bg"], f32).ravel(),
            np.zeros(15, f32)]).reshape(1, 64)),
    }

    nc = build_nc()
    in_maps = []
    for b in range(8):
        m = dict(shared)
        m["xt"] = asc(np.asarray(x[b], f32).T)
        in_maps.append(m)
    res = run_bass_kernel_spmd(nc, in_maps, core_ids=list(range(8)))
    out = np.stack([np.asarray(r["outT"]).T for r in res.results], axis=0)
    if DEBUG_TAPS:
        kernel.tap_results = res.results
    return np.ascontiguousarray(out.astype(np.float32))


# version 06 (final)
# speedup vs baseline: 1.2071x; 1.0350x over previous
import numpy as np
import concourse.bass as bass
import concourse.mybir as mybir
from concourse.bass_utils import run_bass_kernel_spmd

F32 = mybir.dt.float32
BF16 = mybir.dt.bfloat16
AF = mybir.ActivationFunctionType
ALU = mybir.AluOpType
AX = mybir.AxisListType

DEBUG_TAPS = ()
STOP_STAGE = 0
WVPART = 3

T = 2048
D = 1024
KT = 8
HD = 512
H = 8
DK = 64
NCH = 4
CW = 512
NSUB = 4
SW = 128
EPS_ATT = 1e-6
EPS_LN = 1e-5

ENGS = ("sync", "tensor", "vector", "scalar", "gpsimd")


class Builder:
    def __init__(self, nc):
        self.nc = nc
        self.q = {e: [] for e in ENGS}
        self.sems = {}
        self.cnt = {}
        self.last_wait = {e: {} for e in ENGS}

    def sem(self, name):
        if name not in self.sems:
            self.sems[name] = self.nc.alloc_semaphore(name)
            self.cnt[name] = 0
        return self.sems[name]

    def emit(self, eng, fn, waits=(), inc=None, inc_n=1):
        lw = self.last_wait[eng]
        actual = []
        for w in waits:
            if w is None:
                continue
            s, v = w
            if v <= 0 or lw.get(s, 0) >= v:
                continue
            lw[s] = v
            actual.append((s, v))
        tok = None
        if inc is not None:
            self.sem(inc)
            self.cnt[inc] += inc_n
            tok = (inc, self.cnt[inc])
        sems = self.sems

        def run(e, fn=fn, actual=actual, inc=inc, inc_n=inc_n):
            for s, v in actual:
                e.wait_ge(sems[s], v)
            inst = fn(e)
            if inc is not None:
                inst.then_inc(sems[inc], inc_n)
        self.q[eng].append(run)
        return tok


def build_nc():
    nc = bass.Bass("TRN2", target_bir_lowering=False, debug=False)

    xt = nc.dram_tensor("xt", [D, T], F32, kind="ExternalInput").ap()
    wqt = nc.dram_tensor("wqt", [D, HD], F32, kind="ExternalInput").ap()
    wkt = nc.dram_tensor("wkt", [D, HD], F32, kind="ExternalInput").ap()
    wgt = nc.dram_tensor("wgt", [D, HD], F32, kind="ExternalInput").ap()
    wva = nc.dram_tensor("wva", [D, HD], F32, kind="ExternalInput").ap()
    wot = nc.dram_tensor("wot", [HD, D], F32, kind="ExternalInput").ap()
    wvt = nc.dram_tensor("wvt", [D, D], F32, kind="ExternalInput").ap()
    wrp = nc.dram_tensor("wrp", [D, 72], F32, kind="ExternalInput").ap()
    w1t = nc.dram_tensor("w1t", [D, 256], F32, kind="ExternalInput").ap()
    w2t = nc.dram_tensor("w2t", [256, D], F32, kind="ExternalInput").ap()
    b2d = nc.dram_tensor("b2d", [8, D], F32, kind="ExternalInput").ap()
    bvT = nc.dram_tensor("bvT", [128, 8], F32, kind="ExternalInput").ap()
    wgbT = nc.dram_tensor("wgbT", [128, 4], F32, kind="ExternalInput").ap()
    b1cT = nc.dram_tensor("b1cT", [128, 2], F32, kind="ExternalInput").ap()
    smallp = nc.dram_tensor("smallp", [1, 64], F32, kind="ExternalInput").ap()
    outT = nc.dram_tensor("outT", [D, T], F32, kind="ExternalOutput").ap()

    taps = {}
    tapspec = {
        "qf": [128, 2048], "kf": [128, 2048], "gt": [128, 2048],
        "vt": [128, 2080], "at0": [128, 512], "sall": [128, 260],
        "ogfm": [128, 8192], "x1": [128, 16384], "x2": [128, 16384],
        "vfb": [128, 16384], "rc": [128, 2048], "wem": [8, 2048],
        "hp": [128, 4096], "x3": [128, 16384], "qde0": [128, 256],
        "srt": [128, 64], "st1": [128, 32], "grb": [128, 32], "small": [1, 64], "srt2": [128, 64],
        "masks": [128, 1024], "lgb": [128, 8],
    }
    for t_ in DEBUG_TAPS:
        taps[t_] = nc.dram_tensor("tap_" + t_, tapspec[t_], F32 if t_ in
                                  ("sall", "x1", "x2", "rc", "x3", "masks", "lgb", "srt", "st1", "grb", "small", "srt2")
                                  else BF16, kind="ExternalOutput").ap()

    A = nc.alloc_sbuf_tensor
    X = A("X", [128, KT * T], F32)
    RING = A("RING", [128, KT * CW], BF16)
    STG = A("STG", [128, 1024], F32)
    W1REG = A("W1REG", [128, 16384], BF16)
    W2REG = A("W2REG", [128, 4096], BF16)
    W3REG = A("W3REG", [128, 8192], BF16)
    W4REG = A("W4REG", [128, 4096], BF16)
    WRPB = A("WRPB", [128, 576], BF16)
    B2B = A("B2B", [8, 1024], BF16)
    B2S = A("B2S", [8, 128], F32)
    AL1 = A("AL1", [128, 12288], BF16)
    AL1F = AL1.bitcast(F32)
    GT = A("GTb", [128, 2048], BF16)
    VT = A("VTb", [128, 2080], BF16)
    OGB = A("OGB", [64, 2048], BF16)
    ATS = A("ATS", [128, 1024], BF16)
    KDB = A("KDB", [128, 256], BF16)
    QDE = A("QDE", [128, 512], BF16)
    RDB = A("RDBb", [64, 512], BF16)
    SALL = A("SALL", [128, 260], F32)
    SBF = A("SBFb", [128, 520], BF16)
    E1 = A("E1", [128, 512], BF16)
    R1 = A("R1", [128, 512], BF16)
    PIECES = A("PIECES", [128, 1024], BF16)
    MASKS = A("MASKS", [128, 1024], F32)
    QD = A("QD", [128, 512], F32)
    GD2 = A("GD2", [128, 512], F32)
    IO = A("IO", [128, 128], F32)
    TRI = A("TRI", [128, 128], BF16)
    IOD = A("IOD", [128, 64], BF16)
    EYEF = A("EYEF", [128, 128], F32)
    EYEB = A("EYEB", [128, 128], BF16)
    LGB = A("LGB", [128, 8], F32)
    LGP = A("LGP", [128, 4], F32)
    GCP = A("GCP", [128, 4], F32)
    STRS = A("STRS", [1, 512], F32)
    STRB = A("STRB", [1, 1024], BF16)
    ONESC = A("ONESC", [128, 1], BF16)
    ONESR = A("ONESR", [1, 128], BF16)
    ONESRF = A("ONESRF", [1, 128], F32)
    ONEH = A("ONEH", [8, 256], BF16)
    OH44 = A("OH44", [4, 256], BF16)
    RDH = A("RDH", [4, 128], BF16)
    BLK = A("BLK", [64, 8], BF16)
    BLKF = A("BLKF", [64, 8], F32)
    SMALL = A("SMALLs", [1, 64], F32)
    LGROW = A("LGROW", [1, 32], F32)
    BVS = A("BVS", [128, 8], F32)
    WGBS = A("WGBS", [128, 4], F32)
    B1CS = A("B1CS", [128, 2], F32)
    GRB = A("GRB", [128, 32], F32)
    CEPS = A("CEPS", [128, 1], F32)
    CEPS64 = A("CEPS64", [128, 1], F32)
    FILL = A("FILL", [128, 1], F32)
    ST1 = A("ST1", [128, 64], F32)
    SRT = A("SRT", [128, 64], F32)
    WEBF = A("WEBF", [128, 32], BF16)

    P = nc.alloc_psum_tensor
    PB = [P("pb%d" % i, [128, 512], F32) for i in range(7)]
    PTB = P("ptb", [128, 1024], BF16)

    B = Builder(nc)
    for s in ("dx", "dwa", "dwb", "dga", "dgb", "dt", "do",
              "sp", "sv", "ss", "sg"):
        B.sem(s)

    def mm(out_ap, lhsT, rhs, start, stop, waits=()):
        return B.emit("tensor",
                      lambda e, o=out_ap, l=lhsT, r=rhs, st=start, sp_=stop:
                      e.matmul(o, l, r, start=st, stop=sp_),
                      waits=waits, inc="sp")

    def tr(out_ap, in_ap, ident, waits=()):
        return B.emit("tensor",
                      lambda e, o=out_ap, i=in_ap, d=ident:
                      e.transpose(o, i, d),
                      waits=waits, inc="sp")

    def dve(fn, waits=()):
        return B.emit("vector", fn, waits=waits, inc="sv")

    def act(fn, waits=()):
        return B.emit("scalar", fn, waits=waits, inc="ss")

    def gp(fn, waits=()):
        return B.emit("gpsimd", fn, waits=waits, inc="sg")

    def dma(queue, out_ap, in_ap, sem, waits=()):
        return B.emit(queue,
                      lambda e, o=out_ap, i=in_ap: e.dma_start(out=o, in_=i),
                      waits=waits, inc=sem, inc_n=16)

    dma("gpsimd", SMALL[:, :], smallp, "dga")
    dma("gpsimd", BVS[:, :], bvT, "dga")
    dma("gpsimd", WGBS[:, :], wgbT, "dga")
    dma("gpsimd", B1CS[:, :], b1cT, "dga")
    tok_small = ("dga", B.cnt["dga"])
    for kt in range(KT):
        q = "gpsimd" if kt % 2 == 0 else "scalar"
        dma(q, X[:, kt * T:(kt + 1) * T], xt[kt * 128:(kt + 1) * 128, :], "dx")
    tok_x_all = ("dx", B.cnt["dx"])

    gp(lambda e: e.iota(IO[:, :], [[1, 128]], channel_multiplier=-1,
                        allow_small_or_imprecise_dtypes=True))
    gp(lambda e: e.iota(IOD[:, :], [[0, 64]], channel_multiplier=-1, base=127,
                        allow_small_or_imprecise_dtypes=True))
    gp(lambda e: e.tensor_single_scalar(TRI[:, :], IO[:, :], 0.0, ALU.is_ge))
    gp(lambda e: e.tensor_single_scalar(EYEF[:, :], IO[:, :], 0.0, ALU.is_equal))
    gp(lambda e: e.tensor_copy(EYEB[:, :], EYEF[:, :]))
    gp(lambda e: e.memset(ONESC[:, :], 1.0))
    gp(lambda e: e.memset(CEPS[:, :], EPS_LN))
    gp(lambda e: e.memset(CEPS64[:, :], 64.0 * EPS_LN))
    gp(lambda e: e.memset(ONESR[:, :], 1.0))
    gp(lambda e: e.memset(ONESRF[:, :], 1.0))
    gp(lambda e: e.memset(VT[:, :], 1.0))
    gp(lambda e: e.memset(SALL[:, :], 0.0))
    gp(lambda e: e.memset(SBF[:, :], 0.0))
    gp(lambda e: e.iota(BLKF[:, :], [[-8, 8]], channel_multiplier=1,
                        allow_small_or_imprecise_dtypes=True))
    gp(lambda e: e.tensor_single_scalar(BLK[:, :], BLKF[:, :], 7.0, ALU.is_le))
    gp(lambda e: e.tensor_single_scalar(BLKF[:, :], BLKF[:, :], 0.0, ALU.is_ge))
    gp(lambda e: e.tensor_mul(BLK[:, :], BLK[:, :], BLKF[:, :]))
    for ot in range(2):
        gp(lambda e, ot=ot: e.iota(ONEH[0:8, ot * 128:(ot + 1) * 128],
                                   [[1, 4], [0, 32]], channel_multiplier=-1,
                                   base=4 * ot,
                                   allow_small_or_imprecise_dtypes=True))
    gp(lambda e: e.tensor_single_scalar(ONEH[0:8, :], ONEH[0:8, :], 0.0, ALU.is_equal))
    gp(lambda e: e.iota(OH44[0:4, :], [[1, 4], [0, 64]], channel_multiplier=-1,
                        allow_small_or_imprecise_dtypes=True))
    gp(lambda e: e.tensor_single_scalar(OH44[0:4, :], OH44[0:4, :], 0.0, ALU.is_equal))
    tok_consts = ("sg", B.cnt["sg"])

    for hh in range(8):
        dma("gpsimd", B2S[0:8, 0:128], b2d[:, hh * 128:(hh + 1) * 128], "dga",
            waits=[("sg", B.cnt["sg"])] if hh else [])
        tok_b2dma = ("dga", B.cnt["dga"])
        gp(lambda e, hh=hh: e.tensor_copy(B2B[:, hh * 128:(hh + 1) * 128],
                                          B2S[0:8, 0:128]), waits=[tok_b2dma])
    tok_b2 = ("sg", B.cnt["sg"])

    act(lambda e: e.activation(LGROW[0:1, 0:8], SMALL[0:1, 32:40], AF.Sigmoid),
        waits=[tok_small])
    dve(lambda e: e.tensor_scalar_max(LGROW[0:1, 0:8], LGROW[0:1, 0:8], 1e-30),
        waits=[("ss", B.cnt["ss"])])
    act(lambda e: e.activation(LGROW[0:1, 8:16], LGROW[0:1, 0:8], AF.Ln),
        waits=[("sv", B.cnt["sv"])])
    tok_lg = ("ss", B.cnt["ss"])
    mm(PB[6][0:128, 0:8], ONESRF[0:1, :], LGROW[0:1, 8:16], True, True,
       waits=[tok_lg, tok_consts])
    act(lambda e: e.copy(LGB[:, :], PB[6][0:128, 0:8]),
        waits=[("sp", B.cnt["sp"])])
    tok_lgb = ("ss", B.cnt["ss"])
    for p in range(4):
        gp(lambda e, p=p: e.tensor_copy(LGP[0:64, p:p + 1], LGB[0:64, 2 * p:2 * p + 1]),
           waits=[tok_lgb])
        gp(lambda e, p=p: e.tensor_copy(LGP[64:128, p:p + 1],
                                        LGB[64:128, 2 * p + 1:2 * p + 2]))
    tok_lgp = ("sg", B.cnt["sg"])
    act(lambda e: e.activation(GCP[:, :], LGP[:, :], AF.Exp, scale=128.0),
        waits=[tok_lgp])
    tok_gcp = ("ss", B.cnt["ss"])
    def mask_col(h):
        return (h // 4) * 512 + (h % 2) * 256 + ((h % 4) // 2) * 128
    for h in range(H):
        sl = slice(mask_col(h), mask_col(h) + 128)
        dve(lambda e, sl=sl, h=h: e.tensor_scalar(MASKS[:, sl], IO[:, :],
                                                   LGB[:, h:h + 1], 0.0,
                                                   ALU.mult, ALU.min),
            waits=[tok_lgb, tok_consts])
        act(lambda e, sl=sl: e.activation(MASKS[:, sl], MASKS[:, sl], AF.Exp),
            waits=[("sv", B.cnt["sv"])])
        dve(lambda e, sl=sl: e.tensor_mul(MASKS[:, sl], MASKS[:, sl], TRI[:, :]),
            waits=[("ss", B.cnt["ss"])])
    tok_masks = ("sv", B.cnt["sv"])
    for p in range(4):
        sl = slice(p * 128, (p + 1) * 128)
        gp(lambda e, sl=sl: e.iota(QD[:, sl], [[1, 128]], channel_multiplier=0,
                                   base=1, allow_small_or_imprecise_dtypes=True),
           waits=[tok_lgp])
        dve(lambda e, sl=sl, p=p: e.tensor_scalar_mul(QD[:, sl], QD[:, sl],
                                                      LGP[:, p:p + 1]),
            waits=[("sg", B.cnt["sg"])])
        act(lambda e, sl=sl: e.activation(QD[:, sl], QD[:, sl], AF.Exp),
            waits=[("sv", B.cnt["sv"])])
        for half in range(2):
            h = 2 * p + half
            s2 = slice(p * 128 + half * 64, p * 128 + (half + 1) * 64)
            dve(lambda e, s2=s2, h=h: e.tensor_tensor(
                GD2[:, s2], IOD[:, :], LGB[:, h:h + 1].broadcast_to([128, 64]),
                ALU.mult), waits=[tok_lgb, tok_consts])
        act(lambda e, sl=sl: e.activation(GD2[:, sl], GD2[:, sl], AF.Exp),
            waits=[("sv", B.cnt["sv"])])
    tok_qd = ("ss", B.cnt["ss"])

    act(lambda e: e.activation(LGROW[0:1, 30:31], SMALL[0:1, 40:41], AF.Sigmoid),
        waits=[tok_small])
    tok_sa = ("ss", B.cnt["ss"])
    dve(lambda e: e.tensor_scalar_mul(LGROW[0:1, 16:24], SMALL[0:1, 16:24],
                                      LGROW[0:1, 30:31]), waits=[tok_sa])
    dve(lambda e: e.scalar_tensor_tensor(LGROW[0:1, 24:32], SMALL[0:1, 24:32],
                                         LGROW[0:1, 30:31], SMALL[0:1, 8:16],
                                         ALU.mult, ALU.add))
    tok_rows = ("sv", B.cnt["sv"])
    mm(PB[6][0:128, 32:40], ONESRF[0:1, :], SMALL[0:1, 0:8], True, False,
       waits=[tok_rows, ("ss", B.cnt["ss"])])
    mm(PB[6][0:128, 40:48], ONESRF[0:1, :], LGROW[0:1, 16:24], False, False)
    mm(PB[6][0:128, 48:56], ONESRF[0:1, :], LGROW[0:1, 24:32], False, False)
    mm(PB[6][0:128, 56:64], ONESRF[0:1, :], SMALL[0:1, 41:49], False, True)
    act(lambda e: e.copy(GRB[:, :], PB[6][0:128, 32:64]),
        waits=[("sp", B.cnt["sp"])])
    tok_grb = ("ss", B.cnt["ss"])
    sc_slot_toks = {6: [tok_grb]}

    W3F = W3REG.bitcast(F32)
    prev_w3f_cast = None

    def load_big(dram_ap, nrows, cols, dest, dest_off, eng="scalar"):
        nonlocal prev_w3f_cast
        nk = nrows // 128
        src = dram_ap.rearrange("(k p) c -> p k c", p=128)
        w_ = [prev_w3f_cast] if prev_w3f_cast else []
        dma("sync", W3F[:, 0:nk * cols], src, "dwa", waits=w_)
        load_weight_sync.cn["dwa"] += 1
        dtok = ("dwa", load_weight_sync.cn["dwa"] * 16)
        if eng == "scalar":
            ct = act(lambda e, n=nk * cols, o=dest_off: e.copy(
                dest[:, o:o + n], W3F[:, 0:n]), waits=[dtok])
        else:
            ct = gp(lambda e, n=nk * cols, o=dest_off: e.tensor_copy(
                dest[:, o:o + n], W3F[:, 0:n]), waits=[dtok])
        prev_w3f_cast = ct
        return ct

    def load_weight_sync(name, dram_ap, n_rows, cols, dest, dest_off, cast_eng):
        nk = n_rows // 128
        toks = []
        for k in range(nk):
            parity = load_weight_sync.par
            load_weight_sync.par ^= 1
            slot = slice(parity * 512, parity * 512 + cols)
            semn = "dwa" if parity == 0 else "dwb"
            load_weight_sync.cn[semn] += 1
            prev = load_weight_sync.prev[parity]
            dma("sync", STG[:, slot], dram_ap[k * 128:(k + 1) * 128, :], semn,
                waits=[prev] if prev else [])
            dtok = (semn, load_weight_sync.cn[semn] * 16)
            dst = dest[:, dest_off + k * cols: dest_off + (k + 1) * cols]
            if cast_eng == "scalar":
                ct = act(lambda e, d=dst, s=STG[:, slot]: e.copy(d, s), waits=[dtok])
            else:
                ct = gp(lambda e, d=dst, s=STG[:, slot]: e.tensor_copy(d, s), waits=[dtok])
            load_weight_sync.prev[parity] = ct
            toks.append(ct)
        return toks[-1]
    load_weight_sync.par = 0
    load_weight_sync.prev = {0: None, 1: None}
    load_weight_sync.cn = {"dwa": 0, "dwb": 0}


    def load_halves_sync(dram_ap, n_rows, rowcols, dest, dest_off, extra_wait=None,
                         eng="scalar"):
        nk = n_rows // 128
        nh = rowcols // 512
        for k in range(nk):
            for hh in range(nh):
                parity = load_weight_sync.par
                load_weight_sync.par ^= 1
                slot = slice(parity * 512, parity * 512 + 512)
                semn = "dwa" if parity == 0 else "dwb"
                load_weight_sync.cn[semn] += 1
                prev = load_weight_sync.prev[parity]
                dma("sync", STG[:, slot],
                    dram_ap[k * 128:(k + 1) * 128, hh * 512:(hh + 1) * 512], semn,
                    waits=[prev] if prev else [])
                dtok = (semn, load_weight_sync.cn[semn] * 16)
                dst = dest[:, dest_off + k * rowcols + hh * 512:
                           dest_off + k * rowcols + (hh + 1) * 512]
                cw = [dtok]
                if extra_wait is not None:
                    cw.append(extra_wait)
                if eng == "scalar":
                    ct = act(lambda e, d=dst, s=STG[:, slot]: e.copy(d, s), waits=cw)
                else:
                    ct = gp(lambda e, d=dst, s=STG[:, slot]: e.tensor_copy(d, s), waits=cw)
                load_weight_sync.prev[parity] = ct
        return ct




    tok_wq = load_big(wqt, D, 512, W1REG, 0)
    tok_wk = load_big(wkt, D, 512, W1REG, 4096)
    tok_wg = load_big(wgt, D, 512, W1REG, 8192)
    tok_wva = load_big(wva, D, 512, W1REG, 12288)
    tok_wot = tok_w1t = tok_w2t = tok_wrp = tok_wvt = None
    tok_stg_free = None

    def OGfm(sl): return AL1[:, sl]
    QFB = lambda c0, c1: AL1[:, 8192 + c0: 8192 + c1]
    KFB = lambda c0, c1: AL1[:, 10240 + c0: 10240 + c1]

    GEN_BANKS = [0, 2, 3, 6]
    tok_ring = {}
    tok_gate_applied = {}
    tok_ogb_free = {}
    prev_chunk_attn_tok = None
    prev_qf_consumed = None
    prev_gt_consumed = None
    prev_vt_consumed = None

    def bank_free(b):
        return list(sc_slot_toks.get(b, []))

    def set_bank(b, *toks):
        sc_slot_toks[b] = [t for t in toks if t is not None]
    prev_e1_read = {0: None, 1: None}
    prev_at_read = {0: None, 1: None}
    prev_kd_read = {0: None, 1: None}
    prev_qde_read = {0: None, 1: None}
    prev_rdb_read = {0: None}
    prev_rdh_read = {0: None}
    prev_pb6_free = {0: None}
    prev_pb4_rd = {0: None, 1: None}
    prev_webf_tr = {0: None, 1: None}
    prev_sco_evac = {0: None}
    prev_sce_evac = {0: None}
    prev_o_evac = {0: None, 1: None}
    prev_sbf_tok = {0: tok_consts, 1: tok_consts}
    prev_pb4_free = None
    prev_ptb_free = None
    pend_wave = [None]
    tok_sall = tok_consts

    NCH_RUN = NCH if (STOP_STAGE == 0 or STOP_STAGE >= 3) else (1 if STOP_STAGE == 2 else 0)
    prev_chunk_proj_tok = None
    for c in range(NCH_RUN):
        for kt in range(KT):
            w = [tok_x_all]
            if prev_chunk_proj_tok is not None:
                w.append(prev_chunk_proj_tok)
            tok_ring[kt] = gp(lambda e, kt=kt, c=c: e.tensor_copy(
                RING[:, kt * CW:(kt + 1) * CW],
                X[:, kt * T + c * CW: kt * T + (c + 1) * CW]), waits=w)

        for (proj, wtok, woff, dest) in (("q", tok_wq, 0, "QF"), ("k", tok_wk, 4096, "KF")):
            for ot in range(4):
                bank = GEN_BANKS[ot]
                pw = [wtok, tok_ring[7]] + bank_free(bank)
                for kt in range(KT):
                    lhsT = W1REG[:, woff + kt * 512 + ot * 128: woff + kt * 512 + (ot + 1) * 128]
                    rhs = RING[:, kt * CW:(kt + 1) * CW]
                    t_ = mm(PB[bank][:, 0:512], lhsT, rhs, kt == 0, kt == KT - 1,
                            waits=pw + [tok_ring[kt]] if kt == 0 else [tok_ring[kt]])
                mmtok = t_
                ep = 0
                esl = slice(0, 512)
                wE = [mmtok]
                if prev_e1_read[ep]:
                    wE.append(prev_e1_read[ep])
                act(lambda e, esl=esl, b=bank: e.activation(E1[:, esl], PB[b][:, 0:512], AF.Exp),
                    waits=wE)
                act(lambda e, esl=esl, b=bank: e.activation(R1[:, esl], PB[b][:, 0:512], AF.Relu))
                tok_evac = ("ss", B.cnt["ss"])
                dsl = slice(ot * 512, (ot + 1) * 512)
                dst = QFB(dsl.start, dsl.stop) if dest == "QF" else KFB(dsl.start, dsl.stop)
                dw_ = [tok_evac]
                if prev_qf_consumed is not None:
                    dw_.append(prev_qf_consumed)
                stok = dve(lambda e, d=dst, esl=esl: e.scalar_tensor_tensor(
                    d, E1[:, esl], 1.0, R1[:, esl], ALU.min, ALU.add), waits=dw_)
                prev_e1_read[ep] = stok
                set_bank(bank, tok_evac, stok)

        for ot in range(4):
            bank = GEN_BANKS[ot]
            pw = [tok_wg] + bank_free(bank)
            for kt in range(KT):
                lhsT = W1REG[:, 8192 + kt * 512 + ot * 128: 8192 + kt * 512 + (ot + 1) * 128]
                rhs = RING[:, kt * CW:(kt + 1) * CW]
                t_ = mm(PB[bank][:, 0:512], lhsT, rhs, kt == 0, kt == KT - 1,
                        waits=pw if kt == 0 else [])
            gw = [t_]
            if prev_gt_consumed is not None:
                gw.append(prev_gt_consumed)
            gtok = act(lambda e, ot=ot, b=bank: e.activation(
                GT[:, ot * 512:(ot + 1) * 512], PB[b][:, 0:512], AF.Sigmoid,
                bias=WGBS[:, ot:ot + 1]), waits=gw + [tok_small])
            set_bank(bank, gtok)

        for sc_ in range(NSUB):
            bank = GEN_BANKS[sc_]
            pw = [tok_wva] + bank_free(bank)
            for kt in range(KT):
                lhsT = RING[:, kt * CW + sc_ * SW: kt * CW + (sc_ + 1) * SW]
                rhs = W1REG[:, 12288 + kt * 512: 12288 + (kt + 1) * 512]
                t_ = mm(PB[bank][:, 0:512], lhsT, rhs, kt == 0, kt == KT - 1,
                        waits=pw if kt == 0 else [])
            vw = [t_]
            if prev_vt_consumed is not None:
                vw.append(prev_vt_consumed)
            out_ap = bass.AP(VT, sc_ * 520, [[VT.shape[1], 128], [130, 4], [65, 2], [1, 64]])
            in_ap = bass.AP(PB[bank], 0, [[512, 128], [128, 4], [64, 2], [1, 64]])
            vtok = dve(lambda e, o=out_ap, i=in_ap: e.tensor_copy(o, i), waits=vw)
            set_bank(bank, vtok)
        tok_vt_ready = ("sv", B.cnt["sv"])
        tok_qkg_ready = tok_vt_ready
        tok_gt_ready = ("ss", B.cnt["ss"])
        prev_chunk_proj_tok = ("sp", B.cnt["sp"])

        def att_front(sc_, w):
            ctx = {"sc_": sc_, "w": w, "c": c}
            wse = [tok_qkg_ready, tok_masks] 
            if prev_sce_evac[0]:
                wse.append(prev_sce_evac[0])
            wso = []
            if prev_sco_evac[0]:
                wso.append(prev_sco_evac[0])
            for idx in range(2):
                p = 2 * w + idx
                qsl = slice(p * 512 + sc_ * SW, p * 512 + (sc_ + 1) * SW)
                mm(PB[0][:, idx * 128:(idx + 1) * 128],
                   KFB(qsl.start, qsl.stop)[0:64, :],
                   QFB(qsl.start, qsl.stop)[0:64, :], idx == 0, idx == 1,
                   waits=wse if idx == 0 else [])
                tok_sc_o = mm(PB[1][:, idx * 128:(idx + 1) * 128],
                              KFB(qsl.start, qsl.stop)[64:128, :],
                              QFB(qsl.start, qsl.stop)[64:128, :], idx == 0, idx == 1,
                              waits=wso if idx == 0 else [])
            wat = [tok_sc_o]
            if prev_at_read[w]:
                wat.append(prev_at_read[w])
            dve(lambda e, w=w: e.tensor_mul(
                ATS[:, w * 512: w * 512 + 256], PB[0][:, 0:256],
                MASKS[:, w * 512: w * 512 + 256]), waits=wat)
            tok_at_e = ("sv", B.cnt["sv"])
            dve(lambda e, w=w: e.tensor_mul(
                ATS[:, w * 512 + 256: w * 512 + 512], PB[1][:, 0:256],
                MASKS[:, w * 512 + 256: w * 512 + 512]))
            tok_at = ("sv", B.cnt["sv"])
            prev_sce_evac[0] = tok_at_e
            prev_sco_evac[0] = tok_at
            wqd = [tok_qkg_ready, tok_qd]
            if prev_qde_read[w]:
                wqd.append(prev_qde_read[w])
            q_in = bass.AP(AL1, 8192 + 2 * w * 512 + sc_ * SW,
                           [[AL1.shape[1], 128], [512, 2], [1, 128]])
            tok_qde = dve(lambda e, w=w, q_in=q_in: e.tensor_mul(
                QDE[:, w * 256:(w + 1) * 256], q_in,
                QD[:, 2 * w * 128: 2 * w * 128 + 256]), waits=wqd)
            ctx["tok_at"] = tok_at
            ctx["tok_qde"] = tok_qde
            return ctx

        def att_o(ctx):
            sc_, w = ctx["sc_"], ctx["w"]
            for idx in range(2):
                p = 2 * w + idx
                ob = 2 + idx
                wo = [ctx["tok_at"], ctx["tok_qde"], prev_sbf_tok[w], tok_vt_ready]
                if prev_o_evac[idx]:
                    wo.append(prev_o_evac[idx])
                mm(PB[ob][0:64, 0:128], SBF[:, p * 130: p * 130 + 64],
                   QDE[:, w * 256 + idx * 128: w * 256 + (idx + 1) * 128],
                   True, False, waits=wo)
                mm(PB[ob][0:64, 0:128],
                   VT[:, sc_ * 520 + p * 130: sc_ * 520 + p * 130 + 64],
                   ATS[:, w * 512 + idx * 128: w * 512 + (idx + 1) * 128],
                   False, False)
                mm(PB[ob][0:64, 128:256], SBF[:, p * 130 + 65: p * 130 + 129],
                   QDE[:, w * 256 + idx * 128: w * 256 + (idx + 1) * 128],
                   False, False)
                ctx["tok_o%d" % idx] = mm(
                    PB[ob][0:64, 128:256],
                    VT[:, sc_ * 520 + p * 130 + 65: sc_ * 520 + p * 130 + 129],
                    ATS[:, w * 512 + 256 + idx * 128: w * 512 + 256 + (idx + 1) * 128],
                    False, True)

        def att_back(ctx):
            nonlocal prev_ptb_free, tok_sall
            sc_, w, cc = ctx["sc_"], ctx["w"], ctx["c"]
            first = True
            wden = [ctx["tok_at"], ctx["tok_qde"], prev_sbf_tok[w]]
            if prev_pb4_rd[w]:
                wden.append(prev_pb4_rd[w])
            for idx in range(2):
                p = 2 * w + idx
                for side in range(2):
                    h = 4 * w + 2 * idx + side
                    atc = w * 512 + side * 256 + idx * 128
                    mm(PB[4][0:128, h:h + 1], ATS[:, atc:atc + 128], ONESC[:, :],
                       first, False, waits=wden if first else [])
                    first = False
                    zc = p * 130 + 64 + side * 65
                    tok_den = mm(PB[4][0:128, h:h + 1],
                                 QDE[:, w * 256 + idx * 128: w * 256 + (idx + 1) * 128],
                                 SBF[:, zc:zc + 1], False,
                                 (idx == 1 and side == 1))
            wdt = [tok_den]
            if prev_webf_tr[w]:
                wdt.append(prev_webf_tr[w])
            dve(lambda e, w=w: e.tensor_scalar_add(
                SRT[:, w * 4:(w + 1) * 4], PB[4][:, 4 * w: 4 * w + 4], EPS_ATT),
                waits=wdt)
            tok_dt = ("sv", B.cnt["sv"])
            prev_pb4_rd[w] = tok_dt
            dve(lambda e: e.tensor_copy(FILL[:, 0:1], GRB[:, 0:1]))

            def _recip(e, w=w):
                with nc.allow_low_precision(reason="bf16 rd is a matmul operand"):
                    return e.reciprocal(WEBF[:, w * 4:(w + 1) * 4], SRT[:, w * 4:(w + 1) * 4])
            tok_rd = dve(_recip)
            wtr = [tok_rd]
            if prev_ptb_free:
                wtr.append(prev_ptb_free)
            t_tr = tr(PTB[0:4, 0:128], WEBF[:, w * 4:(w + 1) * 4], EYEB[:, :],
                      waits=wtr)
            tok_rdh = act(lambda e: e.copy(RDH[0:4, :], PTB[0:4, 0:128]),
                          waits=[t_tr, prev_rdh_read[0]] if prev_rdh_read[0]
                          else [t_tr])
            prev_ptb_free = tok_rdh
            wbc = [tok_rdh, tok_consts]
            if prev_pb6_free[0]:
                wbc.append(prev_pb6_free[0])
            for hl in range(4):
                tok_bc = mm(PB[6][0:64, hl * 128:(hl + 1) * 128],
                            OH44[0:4, hl * 64:(hl + 1) * 64], RDH[0:4, :],
                            hl == 0, hl == 3, waits=wbc if hl == 0 else [])
            prev_rdh_read[0] = tok_bc
            wrb = [tok_bc]
            if prev_rdb_read[0]:
                wrb.append(prev_rdb_read[0])
            tok_rdb = dve(lambda e: e.tensor_copy(RDB[0:64, 0:512], PB[6][0:64, 0:512]),
                          waits=wrb)
            for idx in range(2):
                p = 2 * w + idx
                ob = 2 + idx
                og_col = p * 2048 + cc * CW + sc_ * SW
                dve(lambda e, ob=ob, og_col=og_col, hl=2 * idx: e.tensor_mul(
                    AL1[0:64, og_col:og_col + 128], PB[ob][0:64, 0:128],
                    RDB[0:64, hl * 128:(hl + 1) * 128]),
                    waits=[tok_rdb, ctx["tok_o%d" % idx]])
                ogb_col = p * 512 + sc_ * 128
                wog = []
                if tok_ogb_free.get(p):
                    wog.append(tok_ogb_free[p])
                tok_og = dve(lambda e, ob=ob, ogb_col=ogb_col, hl=2 * idx + 1: e.tensor_mul(
                    OGB[0:64, ogb_col:ogb_col + 128], PB[ob][0:64, 128:256],
                    RDB[0:64, hl * 128:(hl + 1) * 128]), waits=wog)
                prev_o_evac[idx] = tok_og
            prev_rdb_read[0] = tok_og
            wu0 = [tok_rdb]
            for idx in range(2):
                p = 2 * w + idx
                qsl = slice(p * 512 + sc_ * SW, p * 512 + (sc_ + 1) * SW)
                wtr2 = [prev_ptb_free]
                t_k = tr(PTB[:, 0:128], KFB(qsl.start, qsl.stop), EYEB[:, :],
                         waits=wtr2)
                ksl = slice(idx * 128, (idx + 1) * 128)
                wkd = [t_k, tok_qd]
                if prev_kd_read[idx]:
                    wkd.append(prev_kd_read[idx])
                tok_kd = dve(lambda e, ksl=ksl, p=p: e.tensor_mul(
                    KDB[:, ksl], PTB[:, 0:128], GD2[:, p * 128:(p + 1) * 128]),
                    waits=wkd)
                prev_ptb_free = tok_kd
                tok_u = mm(PB[6][0:128, idx * 130:(idx + 1) * 130], KDB[:, ksl],
                           VT[:, sc_ * 520 + p * 130: sc_ * 520 + (p + 1) * 130],
                           idx == 0, idx == 1,
                           waits=[tok_kd] + wu0 if idx == 0 else [tok_kd])
                prev_kd_read[idx] = tok_u
            for idx in range(2):
                p = 2 * w + idx
                for side in range(2):
                    rows = slice(side * 64, (side + 1) * 64)
                    ucol = idx * 130 + side * 65
                    dve(lambda e, p=p, rows=rows, ucol=ucol: e.scalar_tensor_tensor(
                        SALL[rows, p * 65:(p + 1) * 65], SALL[rows, p * 65:(p + 1) * 65],
                        GCP[rows, p:p + 1], PB[6][rows, ucol:ucol + 65],
                        ALU.mult, ALU.add),
                        waits=[tok_u, tok_gcp, tok_sall])
            tok_sall = ("sv", B.cnt["sv"])
            prev_pb6_free[0] = tok_sall
            for idx in range(2):
                p = 2 * w + idx
                act(lambda e, p=p: e.copy(SBF[0:64, p * 130: p * 130 + 65],
                                          SALL[0:64, p * 65:(p + 1) * 65]),
                    waits=[tok_sall])
                act(lambda e, p=p: e.copy(SBF[64:128, p * 130 + 65: p * 130 + 130],
                                          SALL[64:128, p * 65:(p + 1) * 65]))
            prev_sbf_tok[w] = ("ss", B.cnt["ss"])
            prev_at_read[w] = tok_den
            prev_qde_read[w] = tok_den

        for sc_ in range(NSUB):
            for w in range(2):
                ctxn = att_front(sc_, w)
                if pend_wave[0] is not None:
                    att_back(pend_wave[0])
                att_o(ctxn)
                pend_wave[0] = ctxn
        if pend_wave[0] is not None:
            att_back(pend_wave[0])
            pend_wave[0] = None

        if c == 0:
            tok_wot = load_big(wot, HD, D, W2REG, 0, eng="gpsimd")
            tok_w1t = load_big(w1t, D, 256, W4REG, 0, eng="gpsimd")
            tok_w2t = load_big(w2t, 256, D, W4REG, 2048, eng="gpsimd")
            tok_wrp = load_big(wrp, D, 72, WRPB, 0, eng="gpsimd")
        elif c == 1:
            tok_wvt = load_halves_sync(wvt, D, D, W3REG, 0,
                                       extra_wait=prev_w3f_cast, eng="gpsimd")
            tok_stg_free = ("sg", B.cnt["sg"])

        for p in range(4):
            in_ap = OGB[0:64, p * 512:(p + 1) * 512]
            out_ap = AL1[64:128, p * 2048 + c * CW: p * 2048 + (c + 1) * CW]
            dma("gpsimd", out_ap, in_ap, "dt", waits=[prev_o_evac[p % 2], tok_sall])
            tok_ogb_free[p] = ("dt", B.cnt["dt"])

        tok_stitch = ("dt", B.cnt["dt"])
        for p in range(4):
            tok_gate_applied[(c, p)] = dve(
                lambda e, p=p, c=c: e.tensor_mul(
                    AL1[:, p * 2048 + c * CW: p * 2048 + (c + 1) * CW],
                    AL1[:, p * 2048 + c * CW: p * 2048 + (c + 1) * CW],
                    GT[:, p * 512:(p + 1) * 512]),
                waits=[tok_stitch, tok_gt_ready])
        prev_gt_consumed = ("sv", B.cnt["sv"])
        prev_chunk_attn_tok = ("sp", B.cnt["sp"])
        prev_qf_consumed = prev_chunk_attn_tok
        prev_vt_consumed = prev_chunk_attn_tok
        set_bank(0, prev_sce_evac[0])
        set_bank(1, prev_sco_evac[0])
        set_bank(2, prev_o_evac[0])
        set_bank(3, prev_o_evac[1])
        set_bank(4, prev_pb4_rd[0], prev_pb4_rd[1])
        set_bank(6, prev_pb6_free[0])

        if "qf" in taps and c == 0:
            dma("gpsimd", taps["qf"], AL1[:, 8192:10240], "do", waits=[tok_qkg_ready])
        if "kf" in taps and c == 0:
            dma("gpsimd", taps["kf"], AL1[:, 10240:12288], "do", waits=[tok_qkg_ready])
        if "gt" in taps and c == 0:
            dma("gpsimd", taps["gt"], GT[:, :], "do", waits=[tok_gt_ready])
        if "vt" in taps and c == 0:
            dma("gpsimd", taps["vt"], VT[:, :], "do", waits=[tok_vt_ready])
        if "sall" in taps and c == 0:
            dma("gpsimd", taps["sall"], SALL[:, :], "do", waits=[tok_sall])

    tok_og_done = ("sv", B.cnt["sv"])
    if "grb" in taps:
        dma("gpsimd", taps["grb"], GRB[:, :], "do", waits=[tok_grb])
        dma("gpsimd", taps["small"], SMALL[:, :], "do", waits=[tok_grb])
    if "masks" in taps:
        dma("gpsimd", taps["masks"], MASKS[:, :], "do", waits=[tok_masks])
    if "lgb" in taps:
        dma("gpsimd", taps["lgb"], LGB[:, :], "do", waits=[tok_lgb])
    if "ogfm" in taps:
        dma("gpsimd", taps["ogfm"], AL1[:, 0:8192], "do", waits=[tok_og_done])

    tok_x_attn = {}
    for c in range(NCH if (STOP_STAGE == 0 or STOP_STAGE >= 4) else 0):
        gates = [tok_gate_applied[(c, p)] for p in range(4)]
        for ot in range(KT):
            bank = GEN_BANKS[ot % 4]
            pw = [tok_wot] + gates + bank_free(bank)
            for kt4 in range(4):
                lhsT = W2REG[:, kt4 * 1024 + ot * 128: kt4 * 1024 + (ot + 1) * 128]
                rhs = AL1[:, kt4 * 2048 + c * CW: kt4 * 2048 + (c + 1) * CW]
                t_ = mm(PB[bank][:, 0:512], lhsT, rhs, kt4 == 0, kt4 == 3,
                        waits=pw if kt4 == 0 else [])
            etok = dve(lambda e, ot=ot, c=c, b=bank: e.tensor_add(
                X[:, ot * T + c * CW: ot * T + (c + 1) * CW],
                X[:, ot * T + c * CW: ot * T + (c + 1) * CW],
                PB[b][:, 0:512]), waits=[t_])
            set_bank(bank, etok)
            tok_x_attn[(ot, c)] = etok
    if "x1" in taps:
        dma("gpsimd", taps["x1"], X[:, :], "do", waits=[("sv", B.cnt["sv"])])

    FS = {"prev_ring_pe": None, "prev_piece_pe": {0: None, 1: None},
          "prev_stg_recip": None, "prev_wv_pe": None, "prev_hs_read": None,
          "tok_tapblock": None}
    route_prev_st1 = tok_consts
    route_prev_dve = tok_consts
    route_prev_webf = tok_consts
    tok_x_ln1 = {}
    tok_vfb = {}
    tok_x_vf = {}
    tok_wem = {}
    tok_hp = {}
    tok_x_delta = {}
    HP = lambda sl: AL1[:, sl]
    WEM = AL1[0:8, 4096:6144]
    HS = AL1[:, 6144:6656]
    PVSQ = AL1[0:64, 6656:7168]
    RC = AL1F[:, 4096:6144]
    tok_proj_dead = prev_chunk_attn_tok
    tok_wo_dead = ("sv", B.cnt["sv"])

    def ln_chunk(phase, c, out_dma=False):
        toks_norm = {}
        pm, pq = 2, 3
        for kt in range(KT):
            wc = [tok_x_attn[(kt, c)]] if phase == "ln1" else [tok_x_delta[(kt, c)]]
            if phase == "ln1":
                wc.append(prev_chunk_attn_tok)
            if FS["prev_ring_pe"] is not None:
                wc.append(FS["prev_ring_pe"])
            ctok = act(lambda e, kt=kt, c=c: e.copy(
                RING[:, kt * CW:(kt + 1) * CW],
                X[:, kt * T + c * CW: kt * T + (c + 1) * CW]), waits=wc)
            psl = slice((kt % 2) * 512, (kt % 2) * 512 + 512)
            wsq = [ctok]
            if FS["prev_piece_pe"][kt % 2] is not None:
                wsq.append(FS["prev_piece_pe"][kt % 2])
            sqtok = act(lambda e, kt=kt, psl=psl: e.activation(
                PIECES[:, psl], RING[:, kt * CW:(kt + 1) * CW], AF.Square),
                waits=wsq)
            w0 = []
            if kt == 0:
                w0 += bank_free(pm) + bank_free(pq)
            mm(PB[pm][0:1, 0:512], ONESC[:, :], RING[:, kt * CW:(kt + 1) * CW],
               kt == 0, kt == KT - 1, waits=w0 + [ctok])
            t_q = mm(PB[pq][0:1, 0:512], ONESC[:, :], PIECES[:, psl],
                     kt == 0, kt == KT - 1, waits=[sqtok])
            FS["prev_piece_pe"][kt % 2] = t_q
        FS["prev_ring_pe"] = t_q
        act(lambda e: e.activation(STRS[0:1, 0:512], PB[pm][0:1, 0:512],
                                   AF.Copy, scale=1.0 / D), waits=[t_q])
        act(lambda e: e.activation(STRB[0:1, 0:512], STRS[0:1, 0:512],
                                   AF.Copy, scale=-1.0))
        act(lambda e: e.activation(STRS[0:1, 0:512], STRS[0:1, 0:512],
                                   AF.Square))
        tok_msq = ("ss", B.cnt["ss"])
        dve(lambda e: e.scalar_tensor_tensor(
            STRS[0:1, 0:512], PB[pq][0:1, 0:512], 1.0 / D,
            STRS[0:1, 0:512], ALU.mult, ALU.subtract), waits=[tok_msq])
        tok_var = ("sv", B.cnt["sv"])
        act(lambda e: e.activation(STRS[0:1, 0:512], STRS[0:1, 0:512],
                                   AF.Sqrt, bias=CEPS[0:1, 0:1]), waits=[tok_var])
        tok_sq2 = ("ss", B.cnt["ss"])
        set_bank(pm, tok_sq2)
        set_bank(pq, tok_sq2)
        wcp = [tok_sq2] + ([tok_stg_free] if tok_stg_free else [])
        if FS["prev_stg_recip"] is not None:
            wcp.append(FS["prev_stg_recip"])
        cpt = act(lambda e: e.copy(STG[0:1, 0:512], STRS[0:1, 0:512]), waits=wcp)
        tok_a = dve(lambda e: e.reciprocal(STRS[0:1, 0:512], STG[0:1, 0:512]),
                    waits=[cpt])
        FS["prev_stg_recip"] = tok_a
        act(lambda e: e.activation(STRB[0:1, 512:1024], STRS[0:1, 0:512],
                                   AF.Copy), waits=[tok_a])
        tok_strb = ("ss", B.cnt["ss"])
        bn, ba = 4, 6
        wb = [tok_strb] + bank_free(bn) + bank_free(ba)
        if phase == "ln1" and c == 0:
            wb.append(prev_pb4_free)
        mm(PB[bn][0:128, 0:512], ONESR[0:1, :], STRB[0:1, 0:512], True, True,
           waits=wb)
        t_bc = mm(PB[ba][0:128, 0:512], ONESR[0:1, :], STRB[0:1, 512:1024],
                  True, True)
        last = None
        for kt in range(KT):
            dve(lambda e, kt=kt, c=c: e.tensor_add(
                X[:, kt * T + c * CW: kt * T + (c + 1) * CW],
                X[:, kt * T + c * CW: kt * T + (c + 1) * CW],
                PB[bn][:, 0:512]), waits=[t_bc])
            last = dve(lambda e, kt=kt, c=c: e.tensor_mul(
                X[:, kt * T + c * CW: kt * T + (c + 1) * CW],
                X[:, kt * T + c * CW: kt * T + (c + 1) * CW],
                PB[ba][:, 0:512]))
            toks_norm[(kt, c)] = last
            if out_dma:
                dma("gpsimd", outT[kt * 128:(kt + 1) * 128, c * CW:(c + 1) * CW],
                    X[:, kt * T + c * CW: kt * T + (c + 1) * CW], "do",
                    waits=[last])
        set_bank(bn, last)
        set_bank(ba, last)
        return toks_norm

    def wv_chunk(c):
        for kt in range(KT):
            wv_w = [tok_x_ln1[(kt, c)]]
            if FS["prev_ring_pe"] is not None:
                wv_w.append(FS["prev_ring_pe"])
            ctok = act(lambda e, kt=kt, c=c: e.copy(
                RING[:, kt * CW:(kt + 1) * CW],
                X[:, kt * T + c * CW: kt * T + (c + 1) * CW]),
                waits=wv_w)
            tok_ring[kt] = ctok
        for ot in range(KT):
            bank = GEN_BANKS[ot % 4]
            pw = [tok_wvt] + bank_free(bank)
            for kt in range(KT):
                lhsT = W3REG[:, kt * 1024 + ot * 128: kt * 1024 + (ot + 1) * 128]
                rhs = RING[:, kt * CW:(kt + 1) * CW]
                t_ = mm(PB[bank][:, 0:512], lhsT, rhs, kt == 0, kt == KT - 1,
                        waits=pw + [tok_ring[kt]] if kt == 0 else [tok_ring[kt]])
            vtok = act(lambda e, ot=ot, c=c, b=bank: e.activation(
                W1REG[:, ot * 2048 + c * CW: ot * 2048 + (c + 1) * CW],
                PB[b][:, 0:512], AF.Identity, bias=BVS[:, ot:ot + 1]),
                waits=[t_, tok_proj_dead, tok_small])
            xtok = dve(lambda e, ot=ot, c=c, b=bank: e.tensor_add(
                X[:, ot * T + c * CW: ot * T + (c + 1) * CW],
                X[:, ot * T + c * CW: ot * T + (c + 1) * CW],
                PB[b][:, 0:512]), waits=[t_, vtok])
            xtok2 = dve(lambda e, ot=ot, c=c: e.tensor_scalar_add(
                X[:, ot * T + c * CW: ot * T + (c + 1) * CW],
                X[:, ot * T + c * CW: ot * T + (c + 1) * CW],
                BVS[:, ot:ot + 1]))
            set_bank(bank, vtok, ("sv", B.cnt["sv"]))
            tok_vfb[(ot, c)] = vtok
            tok_x_vf[(ot, c)] = xtok2
        FS["prev_ring_pe"] = ("sp", B.cnt["sp"])

    def route_chunk(c):
        nonlocal prev_ptb_free, route_prev_st1, route_prev_dve, route_prev_webf
        tok_vfb_c = [tok_vfb[(ot, c)] for ot in range(KT)]
        bank = 2
        pw = [tok_wrp, tok_wo_dead] + tok_vfb_c + bank_free(bank)
        for kt in range(KT):
            lhsT = WRPB[:, kt * 72:(kt + 1) * 72]
            rhs = W1REG[:, kt * 2048 + c * CW: kt * 2048 + (c + 1) * CW]
            t_ = mm(PB[bank][0:72, 0:512], lhsT, rhs, kt == 0, kt == KT - 1,
                    waits=pw if kt == 0 else [])
        rtok = act(lambda e, c=c, b=bank: e.copy(
            RC[64:72, c * CW:(c + 1) * CW], PB[b][64:72, 0:512]), waits=[t_])
        sqtok = act(lambda e, b=bank: e.activation(
            PVSQ[:, 0:512], PB[b][0:64, 0:512], AF.Square), waits=[t_])
        set_bank(bank, ("ss", B.cnt["ss"]))
        wcm = [sqtok, tok_consts] + bank_free(3)
        t_c = mm(PB[3][0:8, 0:512], BLK[:, :], PVSQ[:, 0:512], True, True, waits=wcm)
        ctok = act(lambda e, c=c: e.activation(
            RC[0:8, c * CW:(c + 1) * CW], PB[3][0:8, 0:512], AF.Sqrt,
            scale=1.0 / 8.0), waits=[t_c])
        set_bank(3, ctok)

        SRT4 = AL1F[:, 3072:3328]

        def R4(tt, c0, c1):
            return SRT4[:, tt * 64 + c0: tt * 64 + c1]
        cbank = [4, 2, 3, 6]
        wt0 = [rtok, ctok, ("ss", B.cnt["ss"]), route_prev_st1] + \
            bank_free(5) + bank_free(4) + bank_free(2) + bank_free(3) + bank_free(6)
        stoks = []
        for tt in range(4):
            tcol = c * CW + tt * SW
            t_r = tr(PB[5][0:128, 0:8], RC[64:72, tcol:tcol + 128],
                     EYEF[64:72, 64:72], waits=wt0 if tt == 0 else
                     [("ss", B.cnt["ss"])])
            t_c = tr(PB[cbank[tt]][0:128, 0:8], RC[0:8, tcol:tcol + 128],
                     EYEF[0:8, 0:8])
            act(lambda e, tt=tt: e.copy(ST1[:, tt * 16: tt * 16 + 8],
                                        PB[5][0:128, 0:8]), waits=[t_r, t_c])
            stoks.append(act(lambda e, tt=tt, b=cbank[tt]: e.copy(
                ST1[:, tt * 16 + 8: tt * 16 + 16], PB[b][0:128, 0:8])))
        tok_st1 = ("ss", B.cnt["ss"])
        set_bank(5, tok_st1)
        for b_ in cbank:
            set_bank(b_, tok_st1)
        w0 = [tok_st1, tok_grb, tok_consts, route_prev_dve]
        if FS["prev_hs_read"] is not None:
            w0.append(FS["prev_hs_read"])
        for tt in range(4):
            dve(lambda e, tt=tt: e.scalar_tensor_tensor(
                R4(tt, 0, 8), ST1[:, tt * 16: tt * 16 + 8], 1.0, GRB[:, 24:32],
                ALU.mult, ALU.add, accum_out=R4(tt, 56, 57)),
                waits=w0 if tt == 0 else [])
            dve(lambda e, tt=tt: e.tensor_scalar(
                R4(tt, 8, 16), ST1[:, tt * 16 + 8: tt * 16 + 16], 0.0, 0.0,
                ALU.add, ALU.add, accum_out=R4(tt, 57, 58)))
        for tt in range(4):
            dve(lambda e, tt=tt: e.tensor_scalar(R4(tt, 32, 40), R4(tt, 0, 8), 8.0,
                                                 R4(tt, 56, 57), ALU.mult, ALU.subtract))
            dve(lambda e, tt=tt: e.tensor_scalar(R4(tt, 40, 48), R4(tt, 8, 16), 8.0,
                                                 R4(tt, 57, 58), ALU.mult, ALU.subtract))
        tok_d8 = ("sv", B.cnt["sv"])
        for tt in range(4):
            act(lambda e, tt=tt: e.activation(R4(tt, 48, 56), R4(tt, 32, 40),
                                              AF.Square, accum_out=R4(tt, 58, 59)),
                waits=[tok_d8] if tt == 0 else [])
            act(lambda e, tt=tt: e.activation(R4(tt, 48, 56), R4(tt, 40, 48),
                                              AF.Square, accum_out=R4(tt, 59, 60)))
        for tt in range(4):
            act(lambda e, tt=tt: e.activation(R4(tt, 60, 61), R4(tt, 58, 59),
                                              AF.Sqrt, scale=0.125, bias=CEPS64[:, 0:1]))
            act(lambda e, tt=tt: e.activation(R4(tt, 61, 62), R4(tt, 59, 60),
                                              AF.Sqrt, scale=0.125, bias=CEPS64[:, 0:1]))
        tok_sq = ("ss", B.cnt["ss"])
        for tt in range(4):
            dve(lambda e, tt=tt: e.reciprocal(R4(tt, 62, 63), R4(tt, 60, 61)),
                waits=[tok_sq] if tt == 0 else [])
            dve(lambda e, tt=tt: e.reciprocal(R4(tt, 63, 64), R4(tt, 61, 62)))
        for tt in range(4):
            dve(lambda e, tt=tt: e.tensor_scalar_mul(R4(tt, 0, 8), R4(tt, 32, 40),
                                                     R4(tt, 62, 63)))
            dve(lambda e, tt=tt: e.tensor_scalar_mul(R4(tt, 8, 16), R4(tt, 40, 48),
                                                     R4(tt, 63, 64)))
        for tt in range(4):
            dve(lambda e, tt=tt: e.tensor_mul(R4(tt, 16, 24), R4(tt, 0, 8), GRB[:, 0:8]))
            dve(lambda e, tt=tt: e.tensor_mul(R4(tt, 24, 32), R4(tt, 8, 16), GRB[:, 8:16]))
        for tt in range(4):
            dve(lambda e, tt=tt: e.tensor_add(R4(tt, 32, 40), R4(tt, 16, 24),
                                              GRB[:, 16:24]))
        for tt in range(4):
            dve(lambda e, tt=tt: e.tensor_add(R4(tt, 40, 48), R4(tt, 32, 40),
                                              R4(tt, 24, 32)))
        for tt in range(4):
            dve(lambda e, tt=tt: e.tensor_reduce(R4(tt, 56, 57), R4(tt, 40, 48),
                                                 AX.X, ALU.max))
        for tt in range(4):
            dve(lambda e, tt=tt: e.tensor_scalar(R4(tt, 0, 8), R4(tt, 40, 48),
                                                 R4(tt, 56, 57), None, ALU.is_ge))
        for tt in range(4):
            dve(lambda e, tt=tt: e.scalar_tensor_tensor(R4(tt, 8, 16), R4(tt, 0, 8),
                                                        -1e30, R4(tt, 40, 48),
                                                        ALU.mult, ALU.add))
        for tt in range(4):
            dve(lambda e, tt=tt: e.tensor_reduce(R4(tt, 57, 58), R4(tt, 8, 16),
                                                 AX.X, ALU.max))
        for tt in range(4):
            dve(lambda e, tt=tt: e.tensor_scalar_mul(R4(tt, 58, 59), R4(tt, 56, 57),
                                                     -1.0))
        tok_nm1 = ("sv", B.cnt["sv"])
        for tt in range(4):
            act(lambda e, tt=tt: e.activation(R4(tt, 48, 56), R4(tt, 40, 48),
                                              AF.Exp, bias=R4(tt, 58, 59)),
                waits=[tok_nm1] if tt == 0 else [])
        tok_es = ("ss", B.cnt["ss"])
        for tt in range(4):
            dve(lambda e, tt=tt: e.tensor_scalar(R4(tt, 16, 24), R4(tt, 40, 48),
                                                 R4(tt, 57, 58), None, ALU.is_ge),
                waits=[tok_es] if tt == 0 else [])
        for tt in range(4):
            dve(lambda e, tt=tt: e.scalar_tensor_tensor(R4(tt, 24, 32), R4(tt, 48, 56),
                                                        1.0, R4(tt, 16, 24),
                                                        ALU.mult, ALU.mult,
                                                        accum_out=R4(tt, 59, 60)))
        for tt in range(4):
            dve(lambda e, tt=tt: e.reciprocal(R4(tt, 60, 61), R4(tt, 59, 60)))
        wwe = [route_prev_webf]
        for tt in range(4):
            dve(lambda e, tt=tt: e.tensor_scalar_mul(WEBF[:, tt * 8:(tt + 1) * 8],
                                                     R4(tt, 24, 32), R4(tt, 60, 61)),
                waits=wwe if tt == 0 else [])
        tok_we = ("sv", B.cnt["sv"])
        route_prev_dve = tok_we
        for tt in range(4):
            tcol = c * CW + tt * SW
            wtr = [tok_we]
            if prev_ptb_free:
                wtr.append(prev_ptb_free)
            t_wt = tr(PTB[0:8, 0:128], WEBF[:, tt * 8:(tt + 1) * 8], EYEB[:, :],
                      waits=wtr)
            wemtok = act(lambda e, tcol=tcol: e.copy(
                WEM[0:8, tcol:tcol + 128], PTB[0:8, 0:128]), waits=[t_wt])
            prev_ptb_free = wemtok
        route_prev_webf = ("sp", B.cnt["sp"])
        route_prev_st1 = ("sv", B.cnt["sv"])
        tok_wem[c] = ("ss", B.cnt["ss"])

    def w1_chunk(c):
        for ot in range(2):
            bank = 2 + ot
            pw = [tok_w1t] + [tok_vfb[(o_, c)] for o_ in range(KT)] + bank_free(bank)
            for kt in range(KT):
                lhsT = W4REG[:, kt * 256 + ot * 128: kt * 256 + (ot + 1) * 128]
                rhs = W1REG[:, kt * 2048 + c * CW: kt * 2048 + (c + 1) * CW]
                t_ = mm(PB[bank][:, 0:512], lhsT, rhs, kt == 0, kt == KT - 1,
                        waits=pw if kt == 0 else [])
            hw_ = [t_, tok_small]
            if FS["prev_hs_read"] is not None:
                hw_.append(FS["prev_hs_read"])
            htok = act(lambda e, b=bank, ot=ot: e.activation(
                HS[:, 0:512], PB[b][:, 0:512],
                AF.Silu, bias=B1CS[:, ot:ot + 1]), waits=hw_)
            wbank = 4 if ot == 0 else 6
            wwb = [tok_wem[c], tok_consts] + bank_free(wbank)
            t_wb = mm(PB[wbank][0:128, 0:512], ONEH[0:8, ot * 128:(ot + 1) * 128],
                      WEM[0:8, c * CW:(c + 1) * CW], True, True, waits=wwb)
            hptok = dve(lambda e, ot=ot, c=c, wb=wbank: e.tensor_mul(
                AL1[:, ot * 2048 + c * CW: ot * 2048 + (c + 1) * CW],
                HS[:, 0:512], PB[wb][:, 0:512]), waits=[htok, t_wb])
            FS["prev_hs_read"] = hptok
            set_bank(bank, htok)
            set_bank(wbank, hptok)
            tok_hp[(ot, c)] = hptok

    def w2_chunk(c):
        for ot in range(KT):
            bank = GEN_BANKS[ot % 4]
            pw = [tok_w2t, tok_b2, tok_hp[(0, c)], tok_hp[(1, c)], tok_wem[c]] + bank_free(bank)
            for kt2 in range(2):
                lhsT = W4REG[:, 2048 + kt2 * 1024 + ot * 128: 2048 + kt2 * 1024 + (ot + 1) * 128]
                rhs = AL1[:, kt2 * 2048 + c * CW: kt2 * 2048 + (c + 1) * CW]
                mm(PB[bank][:, 0:512], lhsT, rhs, kt2 == 0, False,
                   waits=pw if kt2 == 0 else [])
            t_ = mm(PB[bank][:, 0:512], B2B[0:8, ot * 128:(ot + 1) * 128],
                    WEM[0:8, c * CW:(c + 1) * CW], False, True)
            etok = dve(lambda e, ot=ot, c=c, b=bank: e.tensor_add(
                X[:, ot * T + c * CW: ot * T + (c + 1) * CW],
                X[:, ot * T + c * CW: ot * T + (c + 1) * CW],
                PB[b][:, 0:512]), waits=[t_, tok_x_vf[(ot, c)]])
            set_bank(bank, etok)
            tok_x_delta[(ot, c)] = etok

    STAGE_N = {5: 1, 6: 2, 7: 3, 8: 4, 9: 5}
    nsteps = 6 if STOP_STAGE == 0 else STAGE_N.get(STOP_STAGE, 0)
    for c in range(NCH if nsteps > 0 else 0):
        if nsteps >= 1:
            tok_x_ln1.update(ln_chunk("ln1", c))
        if nsteps >= 2:
            wv_chunk(c)
        if nsteps >= 3:
            route_chunk(c)
        if nsteps >= 4:
            w1_chunk(c)
        if nsteps >= 5:
            w2_chunk(c)
        if nsteps >= 6:
            ln_chunk("ln2", c, out_dma=True)

    tok_vfb_all = ("ss", B.cnt["ss"])
    if "x2" in taps:
        dma("gpsimd", taps["x2"], X[:, :], "do", waits=[("sv", B.cnt["sv"])])
    if "vfb" in taps:
        dma("gpsimd", taps["vfb"], W1REG[:, :], "do", waits=[tok_vfb_all])
    if "rc" in taps:
        dma("gpsimd", taps["rc"], RC[:, :], "do", waits=[("ss", B.cnt["ss"])])
    if "wem" in taps and NCH - 1 in tok_wem:
        dma("gpsimd", taps["wem"], WEM, "do", waits=[tok_wem[NCH - 1]])
    if "hp" in taps:
        dma("gpsimd", taps["hp"], AL1[:, 0:4096], "do", waits=[("sv", B.cnt["sv"])])
    if STOP_STAGE != 0:
        wlast = [("sv", B.cnt["sv"]), ("ss", B.cnt["ss"]),
                 ("sp", B.cnt["sp"]), ("sg", B.cnt["sg"])]
        for kt in range(KT):
            dma("gpsimd", outT[kt * 128:(kt + 1) * 128, :],
                X[:, kt * T:(kt + 1) * T], "do", waits=wlast if kt == 0 else [])
    if "x3" in taps:
        dma("gpsimd", taps["x3"], X[:, :], "do", waits=[("sv", B.cnt["sv"])])

    final_do = B.cnt["do"]
    B.emit("gpsimd", lambda e: e.nop(), waits=[("do", final_do)])

    with nc.Block() as blk:
        @blk.sync
        def _(e):
            for f in B.q["sync"]:
                f(e)

        @blk.tensor
        def _(e):
            for f in B.q["tensor"]:
                f(e)

        @blk.vector
        def _(e):
            for f in B.q["vector"]:
                f(e)

        @blk.scalar
        def _(e):
            for f in B.q["scalar"]:
                f(e)

        @blk.gpsimd
        def _(e):
            for f in B.q["gpsimd"]:
                f(e)

    return nc


def kernel(**inputs):
    x = inputs["L0_attn_x"]
    f32 = np.float32
    asc = np.ascontiguousarray

    shared = {
        "wqt": asc(np.asarray(inputs["L0_attn_W_Q"], f32).T),
        "wkt": asc(np.asarray(inputs["L0_attn_W_K"], f32).T),
        "wgt": asc(np.asarray(inputs["L0_attn_W_g_w"], f32).T),
        "wva": asc(np.asarray(inputs["L0_attn_W_V"], f32).T),
        "wot": asc(np.asarray(inputs["L0_attn_W_O"], f32).T),
        "wvt": asc(np.asarray(inputs["L0_ffn_Wv"], f32).T),
        "wrp": asc(np.concatenate([
            np.asarray(inputs["L0_ffn_Wp"], f32).reshape(64, 1024).T,
            np.asarray(inputs["L0_ffn_Wg"], f32).T], axis=1)),
        "w1t": asc(np.asarray(inputs["L0_ffn_W1"], f32).reshape(256, 1024).T),
        "w2t": asc(np.asarray(inputs["L0_ffn_W2"], f32).transpose(0, 2, 1).reshape(256, 1024)),
        "b2d": asc(np.asarray(inputs["L0_ffn_b2"], f32)),
        "bvT": asc(np.asarray(inputs["L0_ffn_bv"], f32).reshape(8, 128).T),
        "wgbT": asc(np.asarray(inputs["L0_attn_W_g_b"], f32).reshape(4, 128).T),
        "b1cT": asc(np.asarray(inputs["L0_ffn_b1"], f32).reshape(2, 128).T),
        "smallp": asc(np.concatenate([
            np.asarray(inputs["L0_ffn_g_r"], f32).ravel(),
            np.asarray(inputs["L0_ffn_b_r"], f32).ravel(),
            np.asarray(inputs["L0_ffn_g_c"], f32).ravel(),
            np.asarray(inputs["L0_ffn_b_c"], f32).ravel(),
            np.asarray(inputs["L0_attn_decay_log"], f32).ravel(),
            np.asarray(inputs["L0_ffn_alpha"], f32).ravel(),
            np.asarray(inputs["L0_ffn_bg"], f32).ravel(),
            np.zeros(15, f32)]).reshape(1, 64)),
    }

    nc = build_nc()
    in_maps = []
    for b in range(8):
        m = dict(shared)
        m["xt"] = asc(np.asarray(x[b], f32).T)
        in_maps.append(m)
    res = run_bass_kernel_spmd(nc, in_maps, core_ids=list(range(8)))
    out = np.stack([np.asarray(r["outT"]).T for r in res.results], axis=0)
    if DEBUG_TAPS:
        kernel.tap_results = res.results
    return np.ascontiguousarray(out.astype(np.float32))
